# version 01 (final)
# baseline (speedup 1.0000x reference)
import contextlib
import numpy as np
import ml_dtypes
import concourse.bass as bass
import concourse.mybir as mybir
from concourse.bass_utils import run_bass_kernel_spmd

F32 = mybir.dt.float32
BF16 = mybir.dt.bfloat16
AF = mybir.ActivationFunctionType
ALU = mybir.AluOpType

B, S, D = 2, 8192, 2048
INTER, NST, DTR, KC = 4096, 16, 128, 4
DSL = INTER // 4
NH, HD, RD, LAT, EI, NR = 16, 128, 64, 512, 1024, 7
HPC = NH // 4
TC = S // 4
NT = 512
NKT = S // NT
DT = D // 128
GROUPS = [[0, 1, 2, 3], [4, 5, 6, 7]]

N_SUB = 4


def _bf(x):
    return np.ascontiguousarray(x.astype(ml_dtypes.bfloat16))


def _f32(x):
    return np.ascontiguousarray(np.asarray(x, dtype=np.float32))


def make_inmaps(inputs):
    x = _f32(inputs["L0_attn_a_x"])
    in_proj = _f32(inputs["L0_attn_a_in_proj_w"])
    conv_w = _f32(inputs["L0_attn_a_conv_w"]).reshape(INTER, KC)
    conv_b = _f32(inputs["L0_attn_a_conv_b"])
    x_proj = _f32(inputs["L0_attn_a_x_proj_w"])
    dt_w = _f32(inputs["L0_attn_a_dt_proj_w"])
    dt_b = _f32(inputs["L0_attn_a_dt_proj_b"])
    out_proj = _f32(inputs["L0_attn_a_out_proj_w"])
    A_log = _f32(inputs["L0_attn_a_A_log"])
    D_par = _f32(inputs["L0_attn_a_D_param"])
    W_qkv = _f32(inputs["L0_attn_b_W_qkv"])
    W_out = _f32(inputs["L0_attn_b_W_out"])
    ln1 = _f32(inputs["L0_ffn_ln1_w"])
    ln2 = _f32(inputs["L0_ffn_ln2_w"])
    kv_d_w = _f32(inputs["L0_ffn_kv_d_w"])
    q_d_w = _f32(inputs["L0_ffn_q_d_w"])
    k_u_w = _f32(inputs["L0_ffn_k_u_w"])
    v_u_w = _f32(inputs["L0_ffn_v_u_w"])
    q_u_w = _f32(inputs["L0_ffn_q_u_w"])
    rope_k_w = _f32(inputs["L0_ffn_rope_k_w"])
    rope_q_w = _f32(inputs["L0_ffn_rope_q_w"])
    o_w = _f32(inputs["L0_ffn_o_w"])
    sh_g = _f32(inputs["L0_ffn_sh_gate"])
    sh_u = _f32(inputs["L0_ffn_sh_up"])
    sh_d = _f32(inputs["L0_ffn_sh_down"])
    r_g = _f32(inputs["L0_ffn_r_gate"])
    r_u = _f32(inputs["L0_ffn_r_up"])
    r_d = _f32(inputs["L0_ffn_r_down"])
    rt_w = _f32(inputs["L0_ffn_router_w"])
    rt_b = _f32(inputs["L0_ffn_router_b"])

    ones128 = np.ones((1, 128), np.float32)
    inv_freq = 1.0 / (10000.0 ** (np.arange(0, RD, 2, dtype=np.float32) / RD))
    t = np.arange(S, dtype=np.float32)
    ang = np.outer(t, inv_freq)
    emb = np.concatenate([ang, ang], axis=-1)
    rope_cos = np.ascontiguousarray(np.cos(emb).T)
    rope_sin = np.ascontiguousarray(np.sin(emb).T)
    masks = np.zeros((4, 128, NT), np.float32)
    for o in range(4):
        kk = np.arange(128)[:, None] + o * 128
        qq = np.arange(NT)[None, :]
        masks[o] = (kk <= qq).astype(np.float32)

    xTm = [np.ascontiguousarray(x[b]) for b in range(B)]
    ident_f = np.eye(128, dtype=np.float32)
    ind32 = np.zeros((32, 2), np.float32)
    ind32[:16, 0] = 1.0
    ind32[16:, 1] = 1.0
    ind2_32 = np.ascontiguousarray(ind32.T)

    in_maps = []
    for c in range(8):
        b, r = c // 4, c % 4
        hs0 = HPC * r
        dsl = slice(DSL * r, DSL * (r + 1))
        m = {
            "xTm": xTm[b],
            "ident_f": ident_f,
            "ident_b": _bf(ident_f),
            "ind32": ind32,
            "ind2_32": ind2_32,
            "ones_f": ones128,
            "ones_b": _bf(ones128),
            "rope_cos": rope_cos,
            "rope_sin": rope_sin,
            "masks": _bf(masks.reshape(4 * 128, NT)),
            "w_inproj": np.ascontiguousarray(
                np.concatenate([in_proj[:, dsl], in_proj[:, INTER:][:, dsl]], axis=1)),
            "w_conv": np.ascontiguousarray(conv_w[dsl]),
            "b_conv": np.ascontiguousarray(conv_b[dsl].reshape(8, 128)),
            "w_xproj": np.ascontiguousarray(x_proj[dsl]),
            "w_dt": np.ascontiguousarray(dt_w[:, dsl]),
            "b_dt": np.ascontiguousarray(dt_b[dsl].reshape(8, 128)),
            "a_log": np.ascontiguousarray(A_log[dsl]),
            "d_par": np.ascontiguousarray(D_par[dsl].reshape(8, 128)),
            "w_outproj": np.ascontiguousarray(out_proj[dsl]),
            "ln1_w": ln1.reshape(1, D),
            "ln2_w": ln2.reshape(1, D),
            "w_kvd": kv_d_w,
            "w_qd": q_d_w,
            "w_ku": np.ascontiguousarray(k_u_w[:, RD * hs0:RD * (hs0 + HPC)]),
            "w_qu": np.ascontiguousarray(q_u_w[:, RD * hs0:RD * (hs0 + HPC)]),
            "w_ropek": np.ascontiguousarray(rope_k_w[:, RD * hs0:RD * (hs0 + HPC)]),
            "w_ropeq": np.ascontiguousarray(rope_q_w[:, RD * hs0:RD * (hs0 + HPC)]),
            "w_vu": np.ascontiguousarray(v_u_w[:, HD * hs0:HD * (hs0 + HPC)]),
            "w_ow": np.ascontiguousarray(o_w[HD * hs0:HD * (hs0 + HPC)]),
            "w_moe_g": np.ascontiguousarray(np.concatenate([sh_g[None], r_g], 0)),
            "w_moe_u": np.ascontiguousarray(np.concatenate([sh_u[None], r_u], 0)),
            "w_moe_d": np.ascontiguousarray(np.concatenate([sh_d[None], r_d], 0)),
            "w_rt": rt_w,
            "b_rt": rt_b.reshape(1, NR),
            "w_q": np.ascontiguousarray(W_qkv[128 * hs0:128 * hs0 + 512].T),
            "w_k": np.ascontiguousarray(W_qkv[D + 128 * hs0:D + 128 * hs0 + 512].T),
            "w_v": np.ascontiguousarray(W_qkv[2 * D + 128 * hs0:2 * D + 128 * hs0 + 512].T),
            "w_wout": np.ascontiguousarray(W_out.T[128 * hs0:128 * hs0 + 512]),
        }
        in_maps.append(m)
    return in_maps


class P:

    def __init__(self, nc, stack):
        self.nc = nc
        self.stack = stack
        self.sems = {}
        self.cnt = {}

    def sem(self, name):
        if name not in self.sems:
            self.sems[name] = self.stack.enter_context(self.nc.semaphore(name))
            self.cnt[name] = 0
        return self.sems[name]

    def inc(self, instr, name, k=1):
        instr.then_inc(self.sem(name), k)
        self.cnt[name] += k
        return instr

    def val(self, name):
        self.sem(name)
        return self.cnt[name]

    def wait(self, eng, name, val=None):
        eng.wait_ge(self.sem(name), self.cnt[name] if val is None else val)

    def stream(self, base, per):
        return LS(self, base, per)


_AW_OPS = {
    "tensor_tensor", "tensor_scalar", "tensor_reduce", "reduce_max", "reduce_sum",
    "tensor_copy", "memset", "memzero", "reciprocal", "scalar_tensor_tensor",
    "tensor_tensor_scan", "select", "iota", "tensor_add", "tensor_mul", "tensor_sub",
    "tensor_scalar_add", "tensor_scalar_mul", "activation", "copy", "mul", "add",
    "square", "sqrt", "sign", "dma_start",
}


class AW:

    def __init__(self, p, eng, semname):
        self._p = p
        self._e = eng
        self._s = semname

    def __getattr__(self, name):
        f = getattr(self._e, name)
        if name in _AW_OPS:
            def g(*a, **k):
                self._p.wait(self._e, self._s)
                return f(*a, **k)
            return g
        return f


class BlockW:
    def __init__(self, p, block):
        self._p = p
        self._b = block

    def vector(self, fn):
        self._b.vector(lambda e: fn(AW(self._p, e, "DV")))

    def scalar(self, fn):
        self._b.scalar(lambda e: fn(AW(self._p, e, "EV")))

    def gpsimd(self, fn):
        self._b.gpsimd(lambda e: fn(AW(self._p, e, "PL")))

    def tensor(self, fn):
        self._b.tensor(fn)

    def sync(self, fn):
        self._b.sync(fn)


class LS:

    def __init__(self, p, base, per):
        self.p = p
        self.names = [base + "0", base + "1"]
        for n in self.names:
            p.sem(n)
        self.base = [p.cnt[n] for n in self.names]
        self.per = per

    def inc(self, instr, i):
        self.p.inc(instr, self.names[i & 1], 16)

    def done(self, i):
        return self.base[i & 1] + ((i // 2) + 1) * self.per * 16

    def wait(self, eng, i):
        eng.wait_ge(self.p.sem(self.names[i & 1]), self.done(i))




def build_nc():
    nc = bass.Bass("TRN2", target_bir_lowering=False)

    def ein(name, shape, dt=F32):
        return nc.dram_tensor(name, shape, dt, kind="ExternalInput").ap()

    xTm = ein("xTm", [S, D])
    ones_f = ein("ones_f", [1, 128])
    ones_b = ein("ones_b", [1, 128], BF16)
    ident_f_in = ein("ident_f", [128, 128])
    ident_b_in = ein("ident_b", [128, 128], BF16)
    ind32_in = ein("ind32", [32, 2])
    ind2_32_in = ein("ind2_32", [2, 32])
    rope_cos = ein("rope_cos", [RD, S])
    rope_sin = ein("rope_sin", [RD, S])
    masks_in = ein("masks", [4 * 128, NT], BF16)
    w_inproj = ein("w_inproj", [D, 2 * DSL])
    w_conv = ein("w_conv", [DSL, KC])
    b_conv = ein("b_conv", [8, 128])
    w_xproj = ein("w_xproj", [DSL, DTR + 2 * NST])
    w_dt = ein("w_dt", [DTR, DSL])
    b_dt = ein("b_dt", [8, 128])
    a_log = ein("a_log", [DSL, NST])
    d_par = ein("d_par", [8, 128])
    w_outproj = ein("w_outproj", [DSL, D])
    ln1_in = ein("ln1_w", [1, D])
    ln2_in = ein("ln2_w", [1, D])
    w_kvd = ein("w_kvd", [D, LAT])
    w_qd = ein("w_qd", [D, LAT])
    w_ku = ein("w_ku", [LAT, RD * HPC])
    w_qu = ein("w_qu", [LAT, RD * HPC])
    w_ropek = ein("w_ropek", [D, RD * HPC])
    w_ropeq = ein("w_ropeq", [LAT, RD * HPC])
    w_vu = ein("w_vu", [LAT, HD * HPC])
    w_ow = ein("w_ow", [HD * HPC, D])
    w_moe_g = ein("w_moe_g", [8, D, EI])
    w_moe_u = ein("w_moe_u", [8, D, EI])
    w_moe_d = ein("w_moe_d", [8, EI, D])
    w_rt = ein("w_rt", [D, NR])
    b_rt = ein("b_rt", [1, NR])
    w_q = ein("w_q", [D, 512])
    w_k = ein("w_k", [D, 512])
    w_v = ein("w_v", [D, 512])
    w_wout = ein("w_wout", [512, D])

    if N_SUB >= 4:
        y_out = nc.dram_tensor("y", [TC, D], F32, kind="ExternalOutput").ap()
        y_dbg = None
    else:
        y_out = None
        y_dbg = nc.dram_tensor("y_dbg", [S, D], F32, kind="ExternalOutput").ap()

    def dscr(name, shape, dt=F32):
        return nc.dram_tensor(name, shape, dt)

    wb = {}
    wsrc = {"inproj": w_inproj, "xproj": w_xproj, "dtw": w_dt, "outproj": w_outproj,
            "kvd": w_kvd, "qd": w_qd, "ku": w_ku, "qu": w_qu,
            "ropek": w_ropek, "ropeq": w_ropeq, "vu": w_vu, "ow": w_ow,
            "rt": w_rt, "moeg": w_moe_g, "moeu": w_moe_u, "moed": w_moe_d,
            "wq": w_q, "wk": w_k, "wv": w_v, "wout": w_wout}
    for name, src in wsrc.items():
        wb[name] = dscr("wb_" + name, list(src.shape), BF16)

    hg_d = dscr("hg_d", [2 * DSL, S])
    hsf_d = dscr("hsf_d", [DSL, S])
    hsb_d = dscr("hsb_d", [DSL, S], BF16)
    ssmp_p = dscr("ssmp_p", [DTR + 2 * NST, S])
    ssmp_r = dscr("ssmp_r", [DTR + 2 * NST, S])
    bcn_d = dscr("bcn_d", [2 * NST, S], BF16)
    dtf_d = dscr("dtf_d", [DSL, S])
    yf_d = dscr("yf_d", [DSL, S], BF16)
    xlnf_d = dscr("xlnf_d", [D, S], BF16)
    part_d = dscr("part_d", [S, D])
    ch_x = dscr("ch_x", [TC, D])
    ch_h = dscr("ch_h", [TC, D])
    xf_a = dscr("xf_a", [S, D])
    xf_b = dscr("xf_b", [S, D])
    zf_d = dscr("zf_d", [S, D])
    znf_d = dscr("znf_d", [D, S], BF16)
    kvdf_d = dscr("kvdf_d", [LAT, S], BF16)
    qdf_d = dscr("qdf_d", [LAT, S], BF16)
    oh_d = dscr("oh_d", [HD * HPC, S], BF16)
    vf_d = dscr("vf_d", [S, HD * HPC], BF16)
    wt_d = dscr("wt_d", [8, TC])
    hp_d = dscr("hp_d", [8, EI, TC], BF16)

    stack = contextlib.ExitStack()
    with stack:
        p = P(nc, stack)
        block = BlockW(p, stack.enter_context(nc.Block()))

        sb_ones_f = stack.enter_context(nc.sbuf_tensor("c_ones_f", [1, 128], F32))
        sb_ones_b = stack.enter_context(nc.sbuf_tensor("c_ones_b", [1, 128], BF16))
        sb_consts = stack.enter_context(nc.sbuf_tensor("c_consts", [128, 10], F32))
        sb_onecol_b = stack.enter_context(nc.sbuf_tensor("c_onecol_b", [128, 1], BF16))
        sb_masks = stack.enter_context(nc.sbuf_tensor("c_masks", [128, 4 * NT], BF16))
        sb_idf = stack.enter_context(nc.sbuf_tensor("c_idf", [128, 128], F32))
        sb_idb = stack.enter_context(nc.sbuf_tensor("c_idb", [128, 128], BF16))
        sb_ind32 = stack.enter_context(nc.sbuf_tensor("c_ind32", [32, 2], F32))
        sb_ind232 = stack.enter_context(nc.sbuf_tensor("c_ind232", [2, 32], F32))
        sb_ln1 = stack.enter_context(nc.sbuf_tensor("c_ln1", [128, D], F32))
        sb_ln2 = stack.enter_context(nc.sbuf_tensor("c_ln2", [128, D], F32))
        sb_rtb = stack.enter_context(nc.sbuf_tensor("c_rtb", [128, NR], F32))
        sb_bdt = stack.enter_context(nc.sbuf_tensor("c_bdt", [128, 8], F32))
        sb_bcv = stack.enter_context(nc.sbuf_tensor("c_bcv", [128, 8], F32))
        sb_dpar = stack.enter_context(nc.sbuf_tensor("c_dpar", [128, 8], F32))

        def CONST(i):
            return sb_consts[:, i:i + 1]

        ENGS = ["tensor", "vector", "scalar", "gpsimd", "sync"]

        def barrier():
            for en in ENGS:
                p.sem("B" + en)
                p.cnt["B" + en] += 1
            targets = {en: p.cnt["B" + en] for en in ENGS}

            alltgt = [(k, v) for k, v in p.cnt.items()
                      if v > 0 and not k.startswith("B")]

            def mk(en):
                def body(eng):
                    for k, vv in alltgt:
                        eng.wait_ge(p.sems[k], vv)
                    eng.nop().then_inc(p.sem("B" + en), 1)
                    for other in ENGS:
                        if other != en:
                            eng.wait_ge(p.sem("B" + other), targets[other])
                return body
            block.tensor(mk("tensor"))
            block.vector(mk("vector"))
            block.scalar(mk("scalar"))
            block.gpsimd(mk("gpsimd"))
            block.sync(mk("sync"))

        @block.sync
        def _(s):
            p.inc(s.dma_start(out=sb_ones_f[:, :], in_=ones_f), "LD", 16)
            p.inc(s.dma_start(out=sb_ones_b[:, :], in_=ones_b), "LD", 16)
            for v4 in range(4):
                p.inc(s.dma_start(out=sb_masks[:, v4 * NT:(v4 + 1) * NT],
                                  in_=masks_in[v4 * 128:(v4 + 1) * 128, :]), "LD", 16)
            p.inc(s.dma_start(out=sb_idf[:, :], in_=ident_f_in), "LD", 16)
            p.inc(s.dma_start(out=sb_idb[:, :], in_=ident_b_in), "LD", 16)
            p.inc(s.dma_start(out=sb_ind32[:, :], in_=ind32_in), "LD", 16)
            p.inc(s.dma_start(out=sb_ind232[:, :], in_=ind2_32_in), "LD", 16)
            p.inc(s.dma_start(out=sb_ln1[:, :], in_=ln1_in.partition_broadcast(128)), "LD", 16)
            p.inc(s.dma_start(out=sb_ln2[:, :], in_=ln2_in.partition_broadcast(128)), "LD", 16)
            p.inc(s.dma_start(out=sb_rtb[:, :], in_=b_rt.partition_broadcast(128)), "LD", 16)
            for i8 in range(8):
                p.inc(s.dma_start(out=sb_bdt[:, i8:i8 + 1],
                                  in_=b_dt[i8:i8 + 1, :].rearrange("a p -> p a")), "LD", 16)
                p.inc(s.dma_start(out=sb_bcv[:, i8:i8 + 1],
                                  in_=b_conv[i8:i8 + 1, :].rearrange("a p -> p a")), "LD", 16)
                p.inc(s.dma_start(out=sb_dpar[:, i8:i8 + 1],
                                  in_=d_par[i8:i8 + 1, :].rearrange("a p -> p a")), "LD", 16)

        @block.vector
        def _(v):
            for i, val in enumerate([1.0, 1.0 / np.sqrt(128.0), -20.0, 1e-6,
                                     1.0 / 128.0, 0.125, -1.0, 0.25, 1.0 / 16.0, 0.0]):
                p.inc(v.memset(sb_consts[:, i:i + 1], val), "DV", 1)
            p.inc(v.memset(sb_onecol_b[:, :], 1.0), "DV", 1)
        init_dv = p.val("DV")

        cast_after = {}

        @block.gpsimd
        def _(g):
            order = ["inproj", "xproj", "dtw", "outproj", "kvd", "qd", "ku", "qu",
                     "ropek", "ropeq", "vu", "ow", "rt", "moeg", "moeu", "moed",
                     "wq", "wk", "wv", "wout"]
            for name in order:
                src = wsrc[name]
                if len(src.shape) == 3:
                    for e in range(src.shape[0]):
                        p.inc(g.dma_start(out=wb[name].ap()[e], in_=src[e]), "GD", 16)
                else:
                    p.inc(g.dma_start(out=wb[name].ap(), in_=src), "GD", 16)
                cast_after[name] = p.val("GD")
        GD_ALL = p.val("GD")
        for _k in list(cast_after):
            cast_after[_k] = GD_ALL

        def emit_lnt(tag, src_fn, n_tiles, mode, feat_dst, z_dst=None, w_bc=None,
                     t0_of=None):
            if t0_of is None:
                t0_of = lambda i: i * 128
            with contextlib.ExitStack() as eps:
                E = eps.enter_context
                xb = [E(nc.sbuf_tensor(f"{tag}_x{k}", [128, D], F32)) for k in range(2)]
                sqb = [E(nc.sbuf_tensor(f"{tag}_s{k}", [128, D], F32)) for k in range(2)]
                znb = [E(nc.sbuf_tensor(f"{tag}_z{k}", [128, D], BF16)) for k in range(2)]
                zfb = [E(nc.sbuf_tensor(f"{tag}_f{k}", [128, D], F32)) for k in range(2)] \
                    if z_dst is not None else None
                colsb = [E(nc.sbuf_tensor(f"{tag}_c{k}", [128, 12], F32)) for k in range(2)]
                fst = [E(nc.sbuf_tensor(f"{tag}_e{k}", [128, 512], BF16)) for k in range(2)]
                ps_tp = [E(nc.psum_tensor(f"{tag}_p{k}", [128, 512], BF16)) for k in range(2)]

                lsx = p.stream("Lx", 1)
                lszs = p.stream("Lz", 1)
                lsfs = p.stream("Lf", 1)
                LDF = p.val("LD")
                nz = 1 if z_dst is not None else 0
                gsz = 1 if feat_dst[0] == "dram" else 4
                if mode == "ln":
                    ndv, nev, pre = 8 + nz, 2 + 4 * gsz, 2
                elif mode == "ln_rms_w":
                    ndv, nev, pre = 14, 3 + 4 * gsz, 3
                else:
                    ndv, nev, pre = 4, 2 + 4 * gsz, 2
                nmm = 4
                nst = (4 if feat_dst[0] == "dram" else 0) + nz
                M0, V0, E0 = (p.val(x) for x in ["MM", "DV", "EV"])
                sq_done = lambda i: E0 + i * nev + 1
                sqrt1_done = lambda i: E0 + i * nev + 2
                sqrt2_done = lambda i: E0 + i * nev + 3
                evac_done = lambda i, jg: E0 + i * nev + pre + (jg + 1) * gsz
                var_done = lambda i: V0 + i * ndv + (6 if mode == "ln" else (10 if mode == "ln_rms_w" else 2))
                zn_done = lambda i: V0 + (i + 1) * ndv
                tp_done = lambda i, jg: M0 + i * nmm + jg + 1

                def b_sync(s):
                    for i in range(n_tiles):
                        if i >= 2:
                            s.wait_ge(p.sem("DV"), zn_done(i - 2))
                        lsx.inc(s.dma_start(out=xb[i % 2][:, :], in_=src_fn(i)), i)
                block.sync(b_sync)

                def b_scalar(s):
                    s.wait_ge(p.sem("LD"), LDF)
                    for i in range(n_tiles):
                        par = i % 2
                        cols = colsb[par]
                        lsx.wait(s, i)
                        if i >= 2:
                            s.wait_ge(p.sem("DV"), V0 + (i - 2) * ndv + (1 if mode == "rms_w" else 2))
                        p.inc(s.activation(out=sqb[par][:, :], in_=xb[par][:, :],
                                           func=AF.Square), "EV", 1)
                        s.wait_ge(p.sem("DV"), var_done(i))
                        p.inc(s.activation(out=cols[:, 8:9], in_=cols[:, 4:5],
                                           func=AF.Sqrt), "EV", 1)
                        if mode == "ln_rms_w":
                            s.wait_ge(p.sem("DV"), V0 + i * ndv + 10)
                            p.inc(s.activation(out=cols[:, 9:10], in_=cols[:, 7:8],
                                               func=AF.Sqrt), "EV", 1)
                        for jg in range(4):
                            s.wait_ge(p.sem("MM"), tp_done(i, jg))
                            if feat_dst[0] == "dram":
                                gi = i * 4 + jg
                                if gi >= 2:
                                    lsfs.wait(s, gi - 2)
                                p.inc(s.copy(out=fst[gi % 2][:, :], in_=ps_tp[jg % 2][:, :]), "EV", 1)
                                dbuf = feat_dst[1]
                                t0 = t0_of(i)
                                dst = dbuf.rearrange("(g j p) t -> g j p t", j=4, p=128)[
                                    jg, :, :, t0:t0 + 128].rearrange("j p t -> p j t")
                                lsfs.inc(s.dma_start(out=dst, in_=fst[gi % 2][:, :]), gi)
                            else:
                                tile, ncols = feat_dst[1], feat_dst[2]
                                for jj in range(4):
                                    j = jg * 4 + jj
                                    last = s.copy(out=tile[:, j * ncols + t0_of(i):j * ncols + t0_of(i) + 128],
                                                  in_=ps_tp[jg % 2][:, jj * 128:(jj + 1) * 128])
                                    p.inc(last, "EV", 1)
                        if z_dst is not None:
                            s.wait_ge(p.sem("DV"), zn_done(i))
                            lszs.inc(s.dma_start(out=z_dst[i * 128:(i + 1) * 128, :],
                                                 in_=zfb[par][:, :]), i)
                block.scalar(b_scalar)

                def b_vector(v):
                    v.wait_ge(p.sem("LD"), LDF)
                    for i in range(n_tiles):
                        par = i % 2
                        cols = colsb[par]
                        if mode != "rms_w":
                            lsx.wait(v, i)
                            p.inc(v.tensor_reduce(out=cols[:, 0:1], in_=xb[par][:, :],
                                                  axis=mybir.AxisListType.X, op=ALU.add), "DV", 1)
                        if mode == "rms_w":
                            lsx.wait(v, i)
                        v.wait_ge(p.sem("EV"), sq_done(i))
                        p.inc(v.tensor_reduce(out=cols[:, 1:2], in_=sqb[par][:, :],
                                              axis=mybir.AxisListType.X, op=ALU.add), "DV", 1)
                        if mode == "rms_w":
                            p.inc(v.tensor_scalar(out=cols[:, 4:5], in0=cols[:, 1:2],
                                                  scalar1=1.0 / D, scalar2=1e-5,
                                                  op0=ALU.mult, op1=ALU.add), "DV", 1)
                            v.wait_ge(p.sem("EV"), sqrt1_done(i))
                            p.inc(v.reciprocal(out=cols[:, 10:11], in_=cols[:, 8:9]), "DV", 1)
                            if i >= 2:
                                v.wait_ge(p.sem("MM"), tp_done(i - 2, 3))
                            p.inc(v.scalar_tensor_tensor(out=znb[par][:, :], in0=xb[par][:, :],
                                                         scalar=cols[:, 10:11], in1=w_bc,
                                                         op0=ALU.mult, op1=ALU.mult), "DV", 1)
                            continue
                        p.inc(v.tensor_scalar(out=cols[:, 2:3], in0=cols[:, 0:1],
                                              scalar1=1.0 / D, scalar2=None, op0=ALU.mult), "DV", 1)
                        p.inc(v.tensor_scalar(out=cols[:, 3:4], in0=cols[:, 1:2],
                                              scalar1=1.0 / D, scalar2=1e-5,
                                              op0=ALU.mult, op1=ALU.add), "DV", 1)
                        p.inc(v.tensor_tensor(out=cols[:, 5:6], in0=cols[:, 2:3],
                                              in1=cols[:, 2:3], op=ALU.mult), "DV", 1)
                        p.inc(v.tensor_tensor(out=cols[:, 4:5], in0=cols[:, 3:4],
                                              in1=cols[:, 5:6], op=ALU.subtract), "DV", 1)
                        if mode == "ln_rms_w":
                            p.inc(v.tensor_scalar(out=cols[:, 6:7], in0=cols[:, 4:5],
                                                  scalar1=1.0, scalar2=-1e-5,
                                                  op0=ALU.mult, op1=ALU.add), "DV", 1)
                            p.inc(v.reciprocal(out=cols[:, 11:12], in_=cols[:, 4:5]), "DV", 1)
                            p.inc(v.tensor_tensor(out=cols[:, 7:8], in0=cols[:, 6:7],
                                                  in1=cols[:, 11:12], op=ALU.mult), "DV", 1)
                            p.inc(v.tensor_scalar(out=cols[:, 7:8], in0=cols[:, 7:8],
                                                  scalar1=1.0, scalar2=1e-5,
                                                  op0=ALU.mult, op1=ALU.add), "DV", 1)
                        v.wait_ge(p.sem("EV"), sqrt1_done(i))
                        p.inc(v.reciprocal(out=cols[:, 10:11], in_=cols[:, 8:9]), "DV", 1)
                        if i >= 2:
                            v.wait_ge(p.sem("MM"), tp_done(i - 2, 3))
                        if mode == "ln":
                            if z_dst is not None:
                                if i >= 2:
                                    lszs.wait(v, i - 2)
                                p.inc(v.tensor_scalar(out=zfb[par][:, :], in0=xb[par][:, :],
                                                      scalar1=cols[:, 2:3], scalar2=cols[:, 10:11],
                                                      op0=ALU.subtract, op1=ALU.mult), "DV", 1)
                            p.inc(v.tensor_scalar(out=znb[par][:, :], in0=xb[par][:, :],
                                                  scalar1=cols[:, 2:3], scalar2=cols[:, 10:11],
                                                  op0=ALU.subtract, op1=ALU.mult), "DV", 1)
                        else:
                            if i >= 2:
                                lszs.wait(v, i - 2)
                            p.inc(v.tensor_scalar(out=zfb[par][:, :], in0=xb[par][:, :],
                                                  scalar1=cols[:, 2:3], scalar2=cols[:, 10:11],
                                                  op0=ALU.subtract, op1=ALU.mult), "DV", 1)
                            v.wait_ge(p.sem("EV"), sqrt2_done(i))
                            p.inc(v.reciprocal(out=cols[:, 9:10], in_=cols[:, 9:10]), "DV", 1)
                            p.inc(v.scalar_tensor_tensor(out=znb[par][:, :], in0=zfb[par][:, :],
                                                         scalar=cols[:, 9:10], in1=w_bc,
                                                         op0=ALU.mult, op1=ALU.mult), "DV", 1)
                block.vector(b_vector)

                def b_tensor(t):
                    t.wait_ge(p.sem("LD"), LDF)
                    for i in range(n_tiles):
                        par = i % 2
                        t.wait_ge(p.sem("DV"), zn_done(i))
                        for jg in range(4):
                            gi = i * 4 + jg
                            if gi >= 2:
                                jg2, i2 = (jg - 2) % 4, i if jg >= 2 else i - 1
                                t.wait_ge(p.sem("EV"), evac_done(i2, jg2))
                            for jj in range(4):
                                j = jg * 4 + jj
                                mm = t.transpose(ps_tp[jg % 2][:, jj * 128:(jj + 1) * 128],
                                                 znb[par][:, j * 128:(j + 1) * 128], sb_idb[:, :])
                                if jj == 3:
                                    p.inc(mm, "MM", 1)
                block.tensor(b_tensor)

                assert p.val("DV") == V0 + n_tiles * ndv, (tag, p.val("DV"), V0, ndv)
                assert p.val("EV") == E0 + n_tiles * nev
                assert p.val("MM") == M0 + n_tiles * nmm

        def emit_proj(tag, w_dram, Kt, Mt, rhs_fn, out_fn, cast_key=None, mw=128,
                      nct=NKT, evac_bf=True):
            with contextlib.ExitStack() as eps:
                E = eps.enter_context
                rbuf = [E(nc.sbuf_tensor(f"{tag}_r{i}", [128, Kt * NT], BF16)) for i in range(2)]
                wsb = E(nc.sbuf_tensor(f"{tag}_w", [128, Kt * Mt * mw], BF16))
                ost = [E(nc.sbuf_tensor(f"{tag}_o{i}", [mw, NT], BF16 if evac_bf else F32))
                       for i in range(2)]
                pmm = [E(nc.psum_tensor(f"{tag}_p{i}", [mw, NT], F32)) for i in range(4)]

                lsr = p.stream("Pr", Kt)
                lss = p.stream("Ps", 1)

                def wld(s):
                    if cast_key is not None:
                        s.wait_ge(p.sem("GD"), cast_after[cast_key])
                    for k in range(Kt):
                        p.inc(s.dma_start(out=wsb[:, k * Mt * mw:(k + 1) * Mt * mw],
                                          in_=w_dram[k * 128:(k + 1) * 128, :]), "LD", 16)
                block.sync(wld)

                LDF, M0, E0 = (p.val(x) for x in ["LD", "MM", "EV"])
                mm_done = lambda ct, m: M0 + ct * Mt + m + 1
                ev_done = lambda ct, m: E0 + ct * Mt + m + 1

                def b_sync(s):
                    for ct in range(nct):
                        par = ct % 2
                        if ct >= 2:
                            s.wait_ge(p.sem("MM"), mm_done(ct - 2, Mt - 1))
                        for k in range(Kt):
                            lsr.inc(s.dma_start(out=rbuf[par][:, k * NT:(k + 1) * NT],
                                                in_=rhs_fn(k, ct)), ct)
                block.sync(b_sync)

                def b_tensor(t):
                    t.wait_ge(p.sem("LD"), LDF)
                    for ct in range(nct):
                        par = ct % 2
                        lsr.wait(t, ct)
                        for m in range(Mt):
                            gi = ct * Mt + m
                            if gi >= 4:
                                m2, ct2 = (m - 4) % Mt, ct if m >= 4 else ct - 1
                                t.wait_ge(p.sem("EV"), ev_done(ct2, m2))
                            for k in range(Kt):
                                mm = t.matmul(pmm[gi % 4][:, :],
                                              wsb[:, k * Mt * mw + m * mw:k * Mt * mw + (m + 1) * mw],
                                              rbuf[par][:, k * NT:(k + 1) * NT],
                                              start=(k == 0), stop=(k == Kt - 1))
                                if k == Kt - 1:
                                    p.inc(mm, "MM", 1)
                block.tensor(b_tensor)

                def b_scalar(s):
                    for ct in range(nct):
                        for m in range(Mt):
                            gi = ct * Mt + m
                            s.wait_ge(p.sem("MM"), mm_done(ct, m))
                            if gi >= 2:
                                lss.wait(s, gi - 2)
                            p.inc(s.copy(out=ost[gi % 2][:, :], in_=pmm[gi % 4][:, :]), "EV", 1)
                            lss.inc(s.dma_start(out=out_fn(ct, m), in_=ost[gi % 2][:, :]), gi)
                block.scalar(b_scalar)

        def emit_partial(tag, w_dram, Kt, Mt, rhs_fn, res_dram, out_dram, cast_key=None,
                         res2_dram=None, first=True):
            nct = out_dram.shape[0] // NT
            with contextlib.ExitStack() as eps:
                E = eps.enter_context
                rbuf = [E(nc.sbuf_tensor(f"{tag}_r{i}", [128, Kt * NT], BF16)) for i in range(2)]
                wsb = E(nc.sbuf_tensor(f"{tag}_w", [128, Kt * Mt * 128], BF16))
                stg = [E(nc.sbuf_tensor(f"{tag}_g{i}", [128, NT], F32)) for i in range(2)]
                res_sb = [E(nc.sbuf_tensor(f"{tag}_s{i}", [128, NT], F32)) for i in range(2)]
                res2_sb = [E(nc.sbuf_tensor(f"{tag}_s2{i}", [128, NT], F32)) for i in range(2)] \
                    if res2_dram is not None else None
                osb = [E(nc.sbuf_tensor(f"{tag}_o{i}", [128, NT], F32)) for i in range(2)]
                pmm = [E(nc.psum_tensor(f"{tag}_p{i}", [128, NT], F32)) for i in range(2)]
                ptp = [E(nc.psum_tensor(f"{tag}_t{i}", [128, NT], F32)) for i in range(2)]

                def wld(s):
                    if cast_key is not None:
                        s.wait_ge(p.sem("GD"), cast_after[cast_key])
                    for k in range(Kt):
                        p.inc(s.dma_start(out=wsb[:, k * Mt * 128:(k + 1) * Mt * 128],
                                          in_=w_dram[k * 128:(k + 1) * 128, :]), "LD", 16)
                block.sync(wld)

                nres = 2 if res2_dram is not None else 1
                lsr = p.stream("Qr", Kt)
                lse = p.stream("Qe", nres)
                lss = p.stream("Qs", 1)
                LDF, M0, E0, V0 = (p.val(x) for x in ["LD", "MM", "EV", "DV"])
                mm_done = lambda ct, m: M0 + (ct * Mt + m) * 2 + 1
                tp_done = lambda ct, m: M0 + (ct * Mt + m) * 2 + 2
                nvo = 2 if res2_dram is not None else 1
                ev_done = lambda ct, m: E0 + ct * Mt + m + 1
                dv_done = lambda ct, m: V0 + (ct * Mt + m + 1) * nvo

                def tdag(buf, ct, m):
                    return buf.rearrange("(c j q) d -> c q j d", j=4, q=128)[
                        ct, :, :, m * 128:(m + 1) * 128].rearrange("q j d -> q j d")

                def b_sync(s):
                    for ct in range(nct):
                        par = ct % 2
                        if ct >= 2:
                            s.wait_ge(p.sem("MM"), mm_done(ct - 2, Mt - 1))
                        for k in range(Kt):
                            lsr.inc(s.dma_start(out=rbuf[par][:, k * NT:(k + 1) * NT],
                                                in_=rhs_fn(k, ct)), ct)
                        for m in range(Mt):
                            gi = ct * Mt + m
                            if gi >= 2:
                                m2, ct2 = (m - 2) % Mt, ct if m >= 2 else ct - 1
                                s.wait_ge(p.sem("DV"), dv_done(ct2, m2))
                            lse.inc(s.dma_start(out=res_sb[gi % 2][:, :],
                                                in_=tdag(res_dram, ct, m)), gi)
                            if res2_dram is not None:
                                lse.inc(s.dma_start(out=res2_sb[gi % 2][:, :],
                                                    in_=tdag(res2_dram, ct, m)), gi)
                block.sync(b_sync)

                def b_tensor(t):
                    t.wait_ge(p.sem("LD"), LDF)
                    for ct in range(nct):
                        par = ct % 2
                        lsr.wait(t, ct)
                        for m in range(Mt):
                            gi = ct * Mt + m
                            if gi >= 2:
                                m2, ct2 = (m - 2) % Mt, ct if m >= 2 else ct - 1
                                t.wait_ge(p.sem("EV"), ev_done(ct2, m2))
                                t.wait_ge(p.sem("DV"), dv_done(ct2, m2))
                            for k in range(Kt):
                                mm = t.matmul(pmm[gi % 2][:, :],
                                              wsb[:, k * Mt * 128 + m * 128:k * Mt * 128 + (m + 1) * 128],
                                              rbuf[par][:, k * NT:(k + 1) * NT],
                                              start=(k == 0), stop=(k == Kt - 1))
                                if k == Kt - 1:
                                    p.inc(mm, "MM", 1)
                            t.wait_ge(p.sem("EV"), ev_done(ct, m))
                            for jj in range(4):
                                mm = t.transpose(ptp[gi % 2][:, jj * 128:(jj + 1) * 128],
                                                 stg[gi % 2][:, jj * 128:(jj + 1) * 128], sb_idf[:, :])
                                if jj == 3:
                                    p.inc(mm, "MM", 1)
                block.tensor(b_tensor)

                def b_scalar(s):
                    for ct in range(nct):
                        for m in range(Mt):
                            gi = ct * Mt + m
                            s.wait_ge(p.sem("MM"), mm_done(ct, m))
                            if gi >= 2:
                                m2, ct2 = (m - 2) % Mt, ct if m >= 2 else ct - 1
                                s.wait_ge(p.sem("MM"), tp_done(ct2, m2))
                            p.inc(s.copy(out=stg[gi % 2][:, :], in_=pmm[gi % 2][:, :]), "EV", 1)
                            s.wait_ge(p.sem("DV"), dv_done(ct, m))
                            lss.inc(s.dma_start(out=tdag(out_dram, ct, m), in_=osb[gi % 2][:, :]),
                                    gi)
                block.scalar(b_scalar)

                def b_vector(v):
                    for ct in range(nct):
                        for m in range(Mt):
                            gi = ct * Mt + m
                            v.wait_ge(p.sem("MM"), tp_done(ct, m))
                            lse.wait(v, gi)
                            if gi >= 2:
                                lss.wait(v, gi - 2)
                            if res2_dram is None:
                                p.inc(v.scalar_tensor_tensor(out=osb[gi % 2][:, :],
                                                             in0=res_sb[gi % 2][:, :], scalar=0.25,
                                                             in1=ptp[gi % 2][:, :],
                                                             op0=ALU.mult, op1=ALU.add), "DV", 1)
                            else:
                                p.inc(v.tensor_tensor(out=osb[gi % 2][:, :], in0=ptp[gi % 2][:, :],
                                                      in1=res_sb[gi % 2][:, :], op=ALU.add), "DV", 1)
                                p.inc(v.tensor_tensor(out=osb[gi % 2][:, :], in0=osb[gi % 2][:, :],
                                                      in1=res2_sb[gi % 2][:, :], op=ALU.add), "DV", 1)
                block.vector(b_vector)

        def sync_point():
            tgt = [(k, v) for k, v in p.cnt.items()
                   if v > 0 and not k.startswith("B")]

            def body(eng):
                for k, v in tgt:
                    eng.wait_ge(p.sems[k], v)
            for en in ENGS:
                getattr(block, en)(body)

        def attention(tag, mode):
            with contextlib.ExitStack() as eps:
                E = eps.enter_context
                vh_sb = E(nc.sbuf_tensor(f"{tag}_v", [128, 64 * 128], BF16))
                kh = E(nc.sbuf_tensor(f"{tag}_kh", [128, S], BF16))
                qh = E(nc.sbuf_tensor(f"{tag}_qh", [128, S], BF16))
                Ktv = 4 if mode == "mla" else 16
                vl = [E(nc.sbuf_tensor(f"{tag}_vl{i}", [128, Ktv * 128], BF16)) for i in range(2)]
                wv_sb = E(nc.sbuf_tensor(f"{tag}_wv", [128, Ktv * NT], BF16))
                vo = [E(nc.sbuf_tensor(f"{tag}_vo{i}", [128, NT], BF16)) for i in range(2)]
                PS = [E(nc.psum_tensor(f"{tag}_PS{i}", [128, NT], F32)) for i in range(5)]
                ps_v = PS[0:2]
                cs_sb = [E(nc.sbuf_tensor(f"{tag}_cs{i}", [RD, NT], F32)) for i in range(2)]
                sn_sb = [E(nc.sbuf_tensor(f"{tag}_sn{i}", [RD, NT], F32)) for i in range(2)]
                if mode == "mla":
                    wku = E(nc.sbuf_tensor(f"{tag}_wku", [128, 4 * 256], BF16))
                    wqu = E(nc.sbuf_tensor(f"{tag}_wqu", [128, 4 * 256], BF16))
                    wrk = E(nc.sbuf_tensor(f"{tag}_wrk", [128, 16 * 256], BF16))
                    wrq = E(nc.sbuf_tensor(f"{tag}_wrq", [128, 4 * 256], BF16))
                    kv_r = [E(nc.sbuf_tensor(f"{tag}_kvr{i}", [128, 4 * NT], BF16)) for i in range(2)]
                    qd_r = [E(nc.sbuf_tensor(f"{tag}_qdr{i}", [128, 4 * NT], BF16)) for i in range(2)]
                    zn_r = [E(nc.sbuf_tensor(f"{tag}_znr{i}", [128, 16 * NT], BF16)) for i in range(2)]
                    rtmp = E(nc.sbuf_tensor(f"{tag}_rt", [128, 2 * NT], F32))
                    rp = E(nc.sbuf_tensor(f"{tag}_rp", [64, NT], BF16))
                    ps_kq = PS[2:4]
                else:
                    wk_sb = E(nc.sbuf_tensor(f"{tag}_wk", [128, 16 * NT], BF16))
                    wq_sb = E(nc.sbuf_tensor(f"{tag}_wq", [128, 16 * NT], BF16))
                    zn_r = [E(nc.sbuf_tensor(f"{tag}_znr{i}", [128, 16 * NT], BF16)) for i in range(2)]
                    ps_kq = PS[2:4]
                    ko = [E(nc.sbuf_tensor(f"{tag}_ko{i}", [128, NT], BF16)) for i in range(2)]

                def v_wld(s):
                    if mode == "mla":
                        s.wait_ge(p.sem("GD"), cast_after["vu"])
                        for k in range(4):
                            p.inc(s.dma_start(out=wv_sb[:, k * NT:(k + 1) * NT],
                                              in_=wb["vu"].ap()[k * 128:(k + 1) * 128, :]), "LD", 16)
                    else:
                        s.wait_ge(p.sem("GD"), cast_after["wv"])
                        for k in range(16):
                            p.inc(s.dma_start(out=wv_sb[:, k * NT:(k + 1) * NT],
                                              in_=wb["wv"].ap()[k * 128:(k + 1) * 128, :]), "LD", 16)
                block.sync(v_wld)

                lhs_src = kvdf_d if mode == "mla" else znf_d
                lsvl = p.stream("Av", 1)
                lsvs = p.stream("Aw", 1)
                LDF, M0, E0 = p.val("LD"), p.val("MM"), p.val("EV")
                vmm = lambda tm: M0 + tm + 1
                vev = lambda tm: E0 + tm + 1

                def v_sync(s):
                    for tm in range(64):
                        if tm >= 2:
                            s.wait_ge(p.sem("MM"), vmm(tm - 2))
                        src = lhs_src.ap().rearrange("(k q) t -> k q t", q=128)[
                            :, :, tm * 128:(tm + 1) * 128].rearrange("k q t -> q k t")
                        lsvl.inc(s.dma_start(out=vl[tm % 2][:, :], in_=src), tm)
                block.sync(v_sync)

                def v_tensor(t):
                    t.wait_ge(p.sem("LD"), LDF)
                    for tm in range(64):
                        lsvl.wait(t, tm)
                        if tm >= 2:
                            t.wait_ge(p.sem("EV"), vev(tm - 2))
                        for k in range(Ktv):
                            mm = t.matmul(ps_v[tm % 2][:, :], vl[tm % 2][:, k * 128:(k + 1) * 128],
                                          wv_sb[:, k * NT:(k + 1) * NT],
                                          start=(k == 0), stop=(k == Ktv - 1))
                            if k == Ktv - 1:
                                p.inc(mm, "MM", 1)
                block.tensor(v_tensor)

                def v_scalar(s):
                    for tm in range(64):
                        s.wait_ge(p.sem("MM"), vmm(tm))
                        if tm >= 2:
                            lsvs.wait(s, tm - 2)
                        p.inc(s.copy(out=vo[tm % 2][:, :], in_=ps_v[tm % 2][:, :]), "EV", 1)
                        s.wait_ge(p.sem("EV"), vev(tm))
                        lsvs.inc(s.dma_start(out=vf_d.ap()[tm * 128:(tm + 1) * 128, :],
                                             in_=vo[tm % 2][:, :]), tm)
                block.scalar(v_scalar)
                sync_point()

                exps = [E(nc.sbuf_tensor(f"{tag}_e{i}", [128, NT], BF16)) for i in range(2)]
                bc_sb = E(nc.sbuf_tensor(f"{tag}_bc", [128, NT], F32))
                rz = E(nc.sbuf_tensor(f"{tag}_rz", [1, NT], F32))
                ono = [E(nc.sbuf_tensor(f"{tag}_on{i}", [128, NT], BF16)) for i in range(2)]
                ps_s = PS[0:2]
                ps_o = PS[3:5]
                ps_z = PS[2][0:1, :]
                ps_bc = PS[2]

                def kq_wld(s):
                    if mode == "mla":
                        s.wait_ge(p.sem("GD"), cast_after["ropeq"])
                        for k in range(4):
                            p.inc(s.dma_start(out=wku[:, k * 256:(k + 1) * 256],
                                              in_=wb["ku"].ap()[k * 128:(k + 1) * 128, :]), "LD", 16)
                            p.inc(s.dma_start(out=wqu[:, k * 256:(k + 1) * 256],
                                              in_=wb["qu"].ap()[k * 128:(k + 1) * 128, :]), "LD", 16)
                            p.inc(s.dma_start(out=wrq[:, k * 256:(k + 1) * 256],
                                              in_=wb["ropeq"].ap()[k * 128:(k + 1) * 128, :]), "LD", 16)
                        for k in range(16):
                            p.inc(s.dma_start(out=wrk[:, k * 256:(k + 1) * 256],
                                              in_=wb["ropek"].ap()[k * 128:(k + 1) * 128, :]), "LD", 16)
                    else:
                        s.wait_ge(p.sem("GD"), cast_after["wk"])
                        for k in range(16):
                            p.inc(s.dma_start(out=wk_sb[:, k * NT:(k + 1) * NT],
                                              in_=wb["wk"].ap()[k * 128:(k + 1) * 128, :]), "LD", 16)
                            p.inc(s.dma_start(out=wq_sb[:, k * NT:(k + 1) * NT],
                                              in_=wb["wq"].ap()[k * 128:(k + 1) * 128, :]), "LD", 16)
                block.sync(kq_wld)

                for h in range(HPC):
                    sync_point()

                    def vh_ld(s):
                        src = vf_d.ap().rearrange("(tm q) hh -> q tm hh", q=128)[
                            :, :, h * 128:(h + 1) * 128]
                        p.inc(s.dma_start(out=vh_sb[:, :], in_=src), "LD", 16)
                    block.sync(vh_ld)
                    vh_ld_at = p.val("LD")
                    if mode == "mla":
                        lskq = p.stream("Ak", 5)
                        M0, E0, V0, P0 = (p.val(x) for x in ["MM", "EV", "DV", "PL"])
                        kmm = lambda ct, g: M0 + ct * 4 + g + 1
                        kev = lambda ct, i: E0 + ct * 2 + i + 1
                        kdv = lambda ct, i: V0 + (ct * 2 + i + 1) * 6
                        kpl = lambda ct, i: P0 + ct * 2 + i + 1

                        def kq_sync(s):
                            for ct in range(NKT):
                                par = ct % 2
                                if ct >= 2:
                                    s.wait_ge(p.sem("MM"), kmm(ct - 2, 3))
                                    s.wait_ge(p.sem("DV"), kdv(ct - 2, 1))
                                for src, dst, kt in ((kvdf_d, kv_r, 4), (qdf_d, qd_r, 4), (znf_d, zn_r, 16)):
                                    ap = src.ap().rearrange("(k q) t -> k q t", q=128)[
                                        :, :, ct * NT:(ct + 1) * NT].rearrange("k q t -> q k t")
                                    lskq.inc(s.dma_start(out=dst[par][:, :], in_=ap), ct)
                                lskq.inc(s.dma_start(out=cs_sb[par][:, :],
                                                     in_=rope_cos[:, ct * NT:(ct + 1) * NT]), ct)
                                lskq.inc(s.dma_start(out=sn_sb[par][:, :],
                                                     in_=rope_sin[:, ct * NT:(ct + 1) * NT]), ct)
                        block.sync(kq_sync)

                        pk_dst = None

                        def kq_tensor(t):
                            t.wait_ge(p.sem("LD"), LDF)
                            for ct in range(NKT):
                                par = ct % 2
                                lskq.wait(t, ct)
                                for g, (wsb_, rhs, kt) in enumerate((
                                        (wku, kv_r, 4), (wrk, zn_r, 16),
                                        (wqu, qd_r, 4), (wrq, qd_r, 4))):
                                    dst = (ps_kq[0] if g < 2 else ps_kq[1])
                                    dst = dst[64:128, :] if g in (0, 2) else dst[0:64, :]
                                    if ct >= 1:
                                        if g == 0:
                                            t.wait_ge(p.sem("EV"), kev(ct - 1, 0))
                                        elif g == 1:
                                            t.wait_ge(p.sem("DV"), kdv(ct - 1, 0))
                                        elif g == 2:
                                            t.wait_ge(p.sem("EV"), kev(ct - 1, 1))
                                        else:
                                            t.wait_ge(p.sem("DV"), kdv(ct - 1, 1))
                                    for k in range(kt):
                                        mm = t.matmul(dst,
                                                      wsb_[:, k * 256 + h * RD:k * 256 + (h + 1) * RD],
                                                      rhs[par][:, k * NT:(k + 1) * NT],
                                                      start=(k == 0), stop=(k == kt - 1))
                                        if k == kt - 1:
                                            p.inc(mm, "MM", 1)
                        block.tensor(kq_tensor)

                        def kq_scalar(s):
                            for ct in range(NKT):
                                s.wait_ge(p.sem("MM"), kmm(ct, 0))
                                p.inc(s.copy(out=kh[0:64, ct * NT:(ct + 1) * NT],
                                             in_=ps_kq[0][64:128, :]), "EV", 1)
                                s.wait_ge(p.sem("MM"), kmm(ct, 2))
                                p.inc(s.copy(out=qh[0:64, ct * NT:(ct + 1) * NT],
                                             in_=ps_kq[1][64:128, :]), "EV", 1)
                        block.scalar(kq_scalar)

                        def kq_vector(v):
                            for ct in range(NKT):
                                lskq.wait(v, ct)
                                for i, g in enumerate((1, 3)):
                                    v.wait_ge(p.sem("MM"), kmm(ct, g))
                                    if ct * 2 + i >= 1:
                                        i2, ct2 = (i - 1) % 2, ct if i >= 1 else ct - 1
                                        v.wait_ge(p.sem("PL"), kpl(ct2, i2))
                                    pk = ps_kq[0][0:64, :] if g == 1 else ps_kq[1][0:64, :]
                                    cs = cs_sb[ct % 2][:, :]
                                    sn = sn_sb[ct % 2][:, :]
                                    p.inc(v.tensor_tensor(out=rtmp[0:32, 0:NT], in0=pk[0:32, :],
                                                          in1=cs[0:32, :], op=ALU.mult), "DV", 1)
                                    p.inc(v.tensor_tensor(out=rtmp[32:64, 0:NT], in0=pk[32:64, :],
                                                          in1=cs[32:64, :], op=ALU.mult), "DV", 1)
                                    p.inc(v.tensor_tensor(out=rtmp[0:32, NT:2 * NT], in0=pk[32:64, :],
                                                          in1=sn[0:32, :], op=ALU.mult), "DV", 1)
                                    p.inc(v.tensor_tensor(out=rtmp[32:64, NT:2 * NT], in0=pk[0:32, :],
                                                          in1=sn[32:64, :], op=ALU.mult), "DV", 1)
                                    p.inc(v.tensor_tensor(out=rp[0:32, :], in0=rtmp[0:32, 0:NT],
                                                          in1=rtmp[0:32, NT:2 * NT], op=ALU.subtract),
                                          "DV", 1)
                                    p.inc(v.tensor_tensor(out=rp[32:64, :], in0=rtmp[32:64, 0:NT],
                                                          in1=rtmp[32:64, NT:2 * NT], op=ALU.add),
                                          "DV", 1)
                        block.vector(kq_vector)

                        def kq_pool(g):
                            for ct in range(NKT):
                                for i in range(2):
                                    g.wait_ge(p.sem("DV"), kdv(ct, i))
                                    dst = kh if i == 0 else qh
                                    p.inc(g.tensor_copy(out=dst[64:128, ct * NT:(ct + 1) * NT],
                                                        in_=rp[:, :]), "PL", 1)
                        block.gpsimd(kq_pool)
                    else:
                        lskq = p.stream("Ak", 1)
                        M0, E0 = p.val("MM"), p.val("EV")
                        kmm = lambda ct, g: M0 + ct * 2 + g + 1
                        kev = lambda ct, g: E0 + ct * 2 + g + 1

                        def kq_sync(s):
                            for ct in range(NKT):
                                par = ct % 2
                                if ct >= 2:
                                    s.wait_ge(p.sem("MM"), kmm(ct - 2, 1))
                                ap = znf_d.ap().rearrange("(k q) t -> k q t", q=128)[
                                    :, :, ct * NT:(ct + 1) * NT].rearrange("k q t -> q k t")
                                lskq.inc(s.dma_start(out=zn_r[par][:, :], in_=ap), ct)
                        block.sync(kq_sync)

                        def kq_tensor(t):
                            t.wait_ge(p.sem("LD"), LDF)
                            for ct in range(NKT):
                                par = ct % 2
                                lskq.wait(t, ct)
                                for g, wsb_ in enumerate((wk_sb, wq_sb)):
                                    gi = ct * 2 + g
                                    if gi >= 2:
                                        g2, ct2 = (g - 2) % 2, ct if g >= 2 else ct - 1
                                        t.wait_ge(p.sem("EV"), kev(ct2, g2))
                                    for k in range(16):
                                        mm = t.matmul(ps_kq[g % 2][:, :],
                                                      wsb_[:, k * NT + h * 128:k * NT + (h + 1) * 128],
                                                      zn_r[par][:, k * NT:(k + 1) * NT],
                                                      start=(k == 0), stop=(k == 15))
                                        if k == 15:
                                            p.inc(mm, "MM", 1)
                        block.tensor(kq_tensor)

                        def kq_scalar(s):
                            for ct in range(NKT):
                                for g, dst in enumerate((kh, qh)):
                                    s.wait_ge(p.sem("MM"), kmm(ct, g))
                                    p.inc(s.copy(out=dst[:, ct * NT:(ct + 1) * NT],
                                                 in_=ps_kq[g % 2][:, :]), "EV", 1)
                        block.scalar(kq_scalar)

                    sync_point()
                    M0, E0, V0, P0 = (p.val(x) for x in ["MM", "EV", "DV", "PL"])
                    b_base = lambda qt: 2 * qt * (qt + 1)
                    nblk = lambda qt: 4 * qt + 4
                    TOT = b_base(NKT)
                    smm = lambda qt, j: M0 + (b_base(qt) + j) * 3 + qt + 1
                    zmm = lambda qt, j: M0 + (b_base(qt) + j) * 3 + qt + 2
                    amm = lambda qt, j: M0 + (b_base(qt) + j) * 3 + qt + 3
                    bcmm = lambda qt: M0 + b_base(qt + 1) * 3 + qt + 1
                    eexp = lambda qt, j: E0 + b_base(qt) + j + qt + 1
                    ebc = lambda qt: E0 + b_base(qt + 1) + qt + 1
                    vrec = lambda qt: V0 + qt * 2 + 1
                    vono = lambda qt: V0 + qt * 2 + 2
                    pmsk = lambda qt, o: P0 + qt * 4 + o + 1
                    lsos = p.stream("Ao", 1)

                    def bt(t):
                        t.wait_ge(p.sem("LD"), vh_ld_at)
                        for qt in range(NKT):
                            if qt >= 2:
                                t.wait_ge(p.sem("DV"), vono(qt - 2))
                            if qt >= 1:
                                t.wait_ge(p.sem("DV"), vrec(qt - 1))
                                t.wait_ge(p.sem("EV"), ebc(qt - 1))
                            for j in range(nblk(qt)):
                                kt = j
                                gb = b_base(qt) + j
                                if gb >= 2:
                                    j2 = gb - 2
                                    qt2 = qt if j >= 2 else qt - 1
                                    jj2 = j - 2 if j >= 2 else nblk(qt2) + j - 2
                                    t.wait_ge(p.sem("EV"), eexp(qt2, jj2))
                                p.inc(t.matmul(ps_s[gb % 2][:, :], kh[:, kt * 128:(kt + 1) * 128],
                                               qh[:, qt * NT:(qt + 1) * NT],
                                               start=True, stop=True), "MM", 1)
                                if kt >= 4 * qt:
                                    t.wait_ge(p.sem("PL"), pmsk(qt, kt - 4 * qt))
                                else:
                                    t.wait_ge(p.sem("EV"), eexp(qt, j))
                                p.inc(t.matmul(ps_z, sb_onecol_b[:, :], exps[gb % 2][:, :],
                                               start=(j == 0), stop=(j == nblk(qt) - 1)), "MM", 1)
                                p.inc(t.matmul(ps_o[qt % 2][:, :],
                                               vh_sb[:, kt * 128:(kt + 1) * 128],
                                               exps[gb % 2][:, :],
                                               start=(j == 0), stop=(j == nblk(qt) - 1)), "MM", 1)
                            t.wait_ge(p.sem("DV"), vrec(qt))
                            if qt >= 1:
                                t.wait_ge(p.sem("EV"), ebc(qt - 1))
                            p.inc(t.matmul(ps_bc[:, :, ] if False else ps_bc[0:128, :], sb_ones_f[:, :], rz[:, :],
                                           start=True, stop=True), "MM", 1)
                    block.tensor(bt)

                    def bs(s):
                        for qt in range(NKT):
                            for j in range(nblk(qt)):
                                kt = j
                                gb = b_base(qt) + j
                                s.wait_ge(p.sem("MM"), smm(qt, j))
                                if gb >= 2:
                                    qt2 = qt if j >= 2 else qt - 1
                                    jj2 = j - 2 if j >= 2 else nblk(qt2) + j - 2
                                    t2, j2 = (qt2, jj2)
                                    s.wait_ge(p.sem("MM"), amm(t2, j2))
                                    if jj2 >= 4 * qt2:
                                        s.wait_ge(p.sem("PL"), pmsk(qt2, jj2 - 4 * qt2))
                                p.inc(s.activation(out=exps[gb % 2][:, :], in_=ps_s[gb % 2][:, :],
                                                   func=AF.Exp, bias=CONST(2), scale=CONST(1)),
                                      "EV", 1)
                            s.wait_ge(p.sem("MM"), bcmm(qt))
                            if qt >= 1:
                                s.wait_ge(p.sem("DV"), vono(qt - 1))
                            p.inc(s.copy(out=bc_sb[:, :], in_=ps_bc[:, :]), "EV", 1)
                            s.wait_ge(p.sem("DV"), vono(qt))
                            if qt >= 2:
                                lsos.wait(s, qt - 2)
                            lsos.inc(s.dma_start(out=oh_d.ap()[h * 128:(h + 1) * 128, qt * NT:(qt + 1) * NT],
                                                 in_=ono[qt % 2][:, :]), qt)
                    block.scalar(bs)

                    def bp(g):
                        g.wait_ge(p.sem("LD"), LDF)
                        for qt in range(NKT):
                            for o in range(4):
                                j = 4 * qt + o
                                g.wait_ge(p.sem("EV"), eexp(qt, j))
                                p.inc(g.tensor_tensor(out=exps[(b_base(qt) + j) % 2][:, :],
                                                      in0=exps[(b_base(qt) + j) % 2][:, :],
                                                      in1=sb_masks[:, o * NT:(o + 1) * NT],
                                                      op=ALU.mult), "PL", 1)
                    block.gpsimd(bp)

                    def bv(v):
                        for qt in range(NKT):
                            v.wait_ge(p.sem("MM"), zmm(qt, nblk(qt) - 1))
                            if qt >= 1:
                                v.wait_ge(p.sem("MM"), bcmm(qt - 1))
                            p.inc(v.reciprocal(out=rz[:, :], in_=ps_z), "DV", 1)
                            v.wait_ge(p.sem("EV"), ebc(qt))
                            v.wait_ge(p.sem("MM"), amm(qt, nblk(qt) - 1))
                            if qt >= 2:
                                lsos.wait(v, qt - 2)
                            p.inc(v.tensor_tensor(out=ono[qt % 2][:, :], in0=ps_o[qt % 2][:, :],
                                                  in1=bc_sb[:, :], op=ALU.mult), "DV", 1)
                    block.vector(bv)
            sync_point()

        def phase_conv():
            with contextlib.ExitStack() as eps:
                E = eps.enter_context
                hbuf = [E(nc.sbuf_tensor(f"cv_h{i}", [128, S], F32)) for i in range(2)]
                acc1 = E(nc.sbuf_tensor("cv_a", [128, S], F32))
                hs1 = E(nc.sbuf_tensor("cv_s", [128, S], F32))
                acc = [acc1, acc1]
                hs = [hs1, hs1]
                wcv = E(nc.sbuf_tensor("cv_w", [128, 8 * KC], F32))

                lsh = p.stream("cvh", 1)
                lcS = p.stream("cvS", 1)
                lcG = p.stream("cvG", 1)
                V0, E0 = (p.val(x) for x in ["DV", "EV"])

                def b_sync(s):
                    for d8 in range(8):
                        p.inc(s.dma_start(out=wcv[:, d8 * KC:(d8 + 1) * KC],
                                          in_=w_conv[d8 * 128:(d8 + 1) * 128, :]), "LD", 16)
                    for d8 in range(8):
                        if d8 >= 2:
                            s.wait_ge(p.sem("DV"), V0 + (d8 - 1) * KC)
                        lsh.inc(s.dma_start(out=hbuf[d8 % 2][:, :],
                                            in_=hg_d.ap()[d8 * 128:(d8 + 1) * 128, :]), d8)
                block.sync(b_sync)
                LDF = p.val("LD")

                def b_vector(v):
                    v.wait_ge(p.sem("LD"), LDF)
                    for d8 in range(8):
                        par = d8 % 2
                        lsh.wait(v, d8)
                        if d8 >= 1:
                            v.wait_ge(p.sem("EV"), E0 + d8)
                        p.inc(v.tensor_scalar(out=acc[par][:, :], in0=hbuf[par][:, :],
                                              scalar1=wcv[:, d8 * KC + 3:d8 * KC + 4], scalar2=None,
                                              op0=ALU.mult), "DV", 1)
                        for sh in range(1, KC):
                            p.inc(v.scalar_tensor_tensor(out=acc[par][:, sh:S], in0=hbuf[par][:, 0:S - sh],
                                                         scalar=wcv[:, d8 * KC + 3 - sh:d8 * KC + 4 - sh],
                                                         in1=acc[par][:, sh:S], op0=ALU.mult, op1=ALU.add),
                                  "DV", 1)
                block.vector(b_vector)

                def b_scalar(s):
                    s.wait_ge(p.sem("LD"), LDF)
                    for d8 in range(8):
                        par = d8 % 2
                        s.wait_ge(p.sem("DV"), V0 + (d8 + 1) * KC)
                        if d8 >= 1:
                            lcS.wait(s, d8 - 1)
                            lcG.wait(s, d8 - 1)
                        p.inc(s.activation(out=hs[par][:, :], in_=acc[par][:, :], func=AF.Silu,
                                           bias=sb_bcv[:, d8:d8 + 1]), "EV", 1)
                        lcS.inc(s.dma_start(out=hsf_d.ap()[d8 * 128:(d8 + 1) * 128, :],
                                            in_=hs[par][:, :]), d8)
                block.scalar(b_scalar)

                def b_gpsimd(g):
                    for d8 in range(8):
                        g.wait_ge(p.sem("EV"), E0 + d8 + 1)
                        lcG.inc(g.dma_start(out=hsb_d.ap()[d8 * 128:(d8 + 1) * 128, :],
                                            in_=hs[d8 % 2][:, :]), d8)
                block.gpsimd(b_gpsimd)

        def phase_ssmp():
            emit_proj("xp1", wb["xproj"].ap()[:, 0:128], 8, 1,
                      lambda k, ct: hsb_d.ap()[k * 128:(k + 1) * 128, ct * NT:(ct + 1) * NT],
                      lambda ct, m: ssmp_p.ap()[0:128, ct * NT:(ct + 1) * NT],
                      cast_key="xproj", evac_bf=False)
            sync_point()
            emit_proj("xp2", wb["xproj"].ap()[:, 128:160], 8, 1,
                      lambda k, ct: hsb_d.ap()[k * 128:(k + 1) * 128, ct * NT:(ct + 1) * NT],
                      lambda ct, m: ssmp_p.ap()[128:160, ct * NT:(ct + 1) * NT],
                      cast_key="xproj", mw=32, evac_bf=False)
            sync_point()

            def b_g(g):
                p.inc(g.collective_compute("AllReduce", ALU.add, replica_groups=GROUPS,
                                           ins=[ssmp_p.ap().opt()], outs=[ssmp_r.ap().opt()]),
                      "CC", 1)
            block.gpsimd(b_g)
            barrier()

        def phase_dtbc():
            with contextlib.ExitStack() as eps:
                E = eps.enter_context
                dtl = [E(nc.sbuf_tensor(f"db_d{i}", [128, NT], F32)) for i in range(2)]
                bc32 = [E(nc.sbuf_tensor(f"db_b{i}", [32, NT], F32)) for i in range(2)]
                sq1 = E(nc.sbuf_tensor("db_q1", [128, NT], F32))
                sq2 = E(nc.sbuf_tensor("db_q2", [32, NT], F32))
                rows = E(nc.sbuf_tensor("db_r", [2, 4 * NT], F32))
                dtn = [E(nc.sbuf_tensor(f"db_n{i}", [128, NT], BF16)) for i in range(2)]
                bcn = [E(nc.sbuf_tensor(f"db_c{i}", [32, NT], BF16)) for i in range(2)]
                wdt = E(nc.sbuf_tensor("db_w", [128, DSL], BF16))
                spo = [E(nc.sbuf_tensor(f"db_o{i}", [128, NT], F32)) for i in range(2)]
                ps_s1 = E(nc.psum_tensor("db_p1", [1, NT], F32))
                ps_s2 = E(nc.psum_tensor("db_p2", [2, NT], F32))
                ps_b1 = E(nc.psum_tensor("db_p3", [128, NT], F32))
                ps_b2 = E(nc.psum_tensor("db_p4", [32, NT], F32))
                ps_m = [E(nc.psum_tensor(f"db_p5{i}", [128, NT], F32)) for i in range(2)]

                def wld(s):
                    s.wait_ge(p.sem("GD"), cast_after["dtw"])
                    p.inc(s.dma_start(out=wdt[:, :], in_=wb["dtw"].ap()), "LD", 16)
                block.sync(wld)

                lsd = p.stream("dbL", 2)
                lsdS = p.stream("dbS", 1)
                lsdC = p.stream("dbC", 1)
                LDF, M0, V0, E0 = (p.val(x) for x in ["LD", "MM", "DV", "EV"])
                sq_done = lambda ct: E0 + ct * 20 + 2
                sqr_done = lambda ct: E0 + ct * 20 + 4
                sp_done = lambda ct, m: E0 + ct * 20 + 4 + (m + 1) * 2
                stat_done = lambda ct: M0 + ct * 12 + 2
                bc_done = lambda ct: M0 + ct * 12 + 4
                dp_done = lambda ct, m: M0 + ct * 12 + 4 + m + 1
                rec_done = lambda ct: V0 + ct * 4 + 2
                nrm_done = lambda ct: V0 + ct * 4 + 4

                def b_sync(s):
                    for ct in range(NKT):
                        par = ct % 2
                        if ct >= 2:
                            s.wait_ge(p.sem("DV"), nrm_done(ct - 2))
                        lsd.inc(s.dma_start(out=dtl[par][:, :],
                                             in_=ssmp_r.ap()[0:128, ct * NT:(ct + 1) * NT]), ct)
                        lsd.inc(s.dma_start(out=bc32[par][:, :],
                                            in_=ssmp_r.ap()[128:160, ct * NT:(ct + 1) * NT]), ct)
                block.sync(b_sync)

                def b_scalar(s):
                    s.wait_ge(p.sem("LD"), LDF)
                    for ct in range(NKT):
                        par = ct % 2
                        lsd.wait(s, ct)
                        if ct >= 1:
                            s.wait_ge(p.sem("MM"), stat_done(ct - 1))
                        p.inc(s.activation(out=sq1[:, :], in_=dtl[par][:, :], func=AF.Square), "EV", 1)
                        p.inc(s.activation(out=sq2[:, :], in_=bc32[par][:, :], func=AF.Square), "EV", 1)
                        s.wait_ge(p.sem("MM"), stat_done(ct))
                        if ct >= 1:
                            s.wait_ge(p.sem("DV"), rec_done(ct - 1))
                        p.inc(s.activation(out=rows[0:1, NT:NT + NT], in_=ps_s1[:, :], func=AF.Sqrt,
                                           scale=CONST(4)[0:1, :], bias=CONST(3)[0:1, :]), "EV", 1)
                        p.inc(s.activation(out=rows[0:2, 2 * NT:3 * NT], in_=ps_s2[:, :], func=AF.Sqrt,
                                           scale=CONST(8)[0:2, :], bias=CONST(3)[0:2, :]), "EV", 1)
                        for m in range(8):
                            s.wait_ge(p.sem("MM"), dp_done(ct, m))
                            if ct * 8 + m >= 2:
                                lsdS.wait(s, ct * 8 + m - 2)
                            p.inc(s.activation(out=spo[m % 2][:, :], in_=ps_m[m % 2][:, :],
                                               func=AF.Exp, bias=sb_bdt[:, m:m + 1]), "EV", 1)
                            p.inc(s.activation(out=spo[m % 2][:, :], in_=spo[m % 2][:, :],
                                               func=AF.Ln, bias=CONST(0)), "EV", 1)
                            lsdS.inc(s.dma_start(out=dtf_d.ap()[m * 128:(m + 1) * 128, ct * NT:(ct + 1) * NT],
                                                 in_=spo[m % 2][:, :]), ct * 8 + m)
                        s.wait_ge(p.sem("DV"), nrm_done(ct))
                        lsdC.inc(s.dma_start(out=bcn_d.ap()[:, ct * NT:(ct + 1) * NT],
                                             in_=bcn[ct % 2][:, :]), ct)
                block.scalar(b_scalar)

                def b_tensor(t):
                    t.wait_ge(p.sem("LD"), LDF)
                    for ct in range(NKT):
                        par = ct % 2
                        t.wait_ge(p.sem("EV"), sq_done(ct))
                        if ct >= 1:
                            t.wait_ge(p.sem("EV"), sqr_done(ct - 1))
                        p.inc(t.matmul(ps_s1[:, :], CONST(0), sq1[:, :], start=True, stop=True),
                              "MM", 1)
                        p.inc(t.matmul(ps_s2[:, :], sb_ind32[:, :], sq2[:, :], start=True, stop=True),
                              "MM", 1)
                        t.wait_ge(p.sem("DV"), rec_done(ct))
                        if ct >= 1:
                            t.wait_ge(p.sem("DV"), nrm_done(ct - 1))
                        p.inc(t.matmul(ps_b1[:, :], sb_ones_f[:, :], rows[0:1, 3 * NT:4 * NT],
                                       start=True, stop=True), "MM", 1)
                        p.inc(t.matmul(ps_b2[:, :], sb_ind232[:, :], rows[0:2, 0:NT],
                                       start=True, stop=True), "MM", 1)
                        t.wait_ge(p.sem("DV"), nrm_done(ct))
                        for m in range(8):
                            if ct * 8 + m >= 2:
                                m2, ct2 = (m - 2) % 8, ct if m >= 2 else ct - 1
                                t.wait_ge(p.sem("EV"), sp_done(ct2, m2))
                            p.inc(t.matmul(ps_m[m % 2][:, :], wdt[:, m * 128:(m + 1) * 128],
                                           dtn[par][:, :], start=True, stop=True), "MM", 1)
                block.tensor(b_tensor)

                def b_vector(v):
                    v.wait_ge(p.sem("LD"), LDF)
                    for ct in range(NKT):
                        par = ct % 2
                        v.wait_ge(p.sem("EV"), sqr_done(ct))
                        p.inc(v.reciprocal(out=rows[0:1, 3 * NT:4 * NT], in_=rows[0:1, NT:2 * NT]),
                              "DV", 1)
                        p.inc(v.reciprocal(out=rows[0:2, 0:NT], in_=rows[0:2, 2 * NT:3 * NT]),
                              "DV", 1)
                        v.wait_ge(p.sem("MM"), bc_done(ct))
                        if ct >= 2:
                            lsdC.wait(v, ct - 2)
                        if ct >= 1:
                            v.wait_ge(p.sem("MM"), dp_done(ct - 1, 7))
                        p.inc(v.tensor_tensor(out=dtn[par][:, :], in0=dtl[par][:, :],
                                              in1=ps_b1[:, :], op=ALU.mult), "DV", 1)
                        p.inc(v.tensor_tensor(out=bcn[par][:, :], in0=bc32[par][:, :],
                                              in1=ps_b2[:, :], op=ALU.mult), "DV", 1)
                block.vector(b_vector)
            barrier()

        def phase_scan():
            TSC = 1024
            NCH = S // TSC
            with contextlib.ExitStack() as eps:
                E = eps.enter_context
                a_sb = E(nc.sbuf_tensor("sc_a", [128, 128], F32))
                a_neg = E(nc.sbuf_tensor("sc_an", [128, 128], F32))
                carry = E(nc.sbuf_tensor("sc_cy", [128, 128], F32))
                Bb = E(nc.sbuf_tensor("sc_B", [128, NST * TSC], BF16))
                Cb = E(nc.sbuf_tensor("sc_C", [128, NST * TSC], BF16))
                dtb = [E(nc.sbuf_tensor(f"sc_dt{i}", [128, TSC], F32)) for i in range(2)]
                ub = [E(nc.sbuf_tensor(f"sc_u{i}", [128, TSC], F32)) for i in range(2)]
                gb = [E(nc.sbuf_tensor(f"sc_g{i}", [128, TSC], F32)) for i in range(2)]
                sg = [E(nc.sbuf_tensor(f"sc_sg{i}", [128, TSC], F32)) for i in range(2)]
                dtu = E(nc.sbuf_tensor("sc_du", [128, TSC], F32))
                dA = [E(nc.sbuf_tensor(f"sc_dA{i}", [128, TSC], F32)) for i in range(2)]
                dBu = [E(nc.sbuf_tensor(f"sc_dB{i}", [128, TSC], F32)) for i in range(2)]
                sbf = [E(nc.sbuf_tensor(f"sc_s{i}", [128, TSC], F32)) for i in range(2)]
                tmp = [E(nc.sbuf_tensor(f"sc_t{i}", [128, TSC], F32)) for i in range(2)]
                yac = E(nc.sbuf_tensor("sc_y", [128, TSC], F32))
                yo = [E(nc.sbuf_tensor(f"sc_yo{i}", [128, TSC], BF16)) for i in range(2)]

                lsB = p.stream("scB", 32)
                lsG = p.stream("scG", 3)
                lsY = p.stream("scY", 1)
                V0, E0, P0 = (p.val(x) for x in ["DV", "EV", "PL"])
                sdv = lambda g, n: V0 + 2 + g * 51 + 2 + 3 * n
                ydv = lambda g, n: V0 + 2 + g * 51 + 4 + 3 * n
                dtu_dv = lambda g: V0 + 2 + g * 51 + 1
                yfin_dv = lambda g: V0 + 2 + (g + 1) * 51
                adv = lambda g, n: E0 + 1 + g * 17 + n + 1
                silu_ev = lambda g: E0 + 1 + (g + 1) * 17
                bdu_pl = lambda g, n: P0 + g * 32 + 2 * n + 1
                tmp_pl = lambda g, n: P0 + g * 32 + 2 * n + 2
                def b_sync(s):
                    s.wait_ge(p.sem("GD"), cast_after["wout"])
                    for d8 in range(8):
                        p.inc(s.dma_start(out=a_sb[:, d8 * 16:(d8 + 1) * 16],
                                          in_=a_log[d8 * 128:(d8 + 1) * 128, :]), "LD", 16)
                    for c in range(NCH):
                        if c >= 1:
                            s.wait_ge(p.sem("PL"), P0 + c * 8 * 32)
                        for n in range(NST):
                            lsB.inc(s.dma_start(out=Bb[:, n * TSC:(n + 1) * TSC],
                                                in_=bcn_d.ap()[n:n + 1, c * TSC:(c + 1) * TSC].partition_broadcast(128)),
                                    c)
                            lsB.inc(s.dma_start(out=Cb[:, n * TSC:(n + 1) * TSC],
                                                in_=bcn_d.ap()[NST + n:NST + n + 1, c * TSC:(c + 1) * TSC].partition_broadcast(128)),
                                    c)
                        for d8 in range(8):
                            g = c * 8 + d8
                            if g >= 2:
                                s.wait_ge(p.sem("DV"), yfin_dv(g - 2))
                                s.wait_ge(p.sem("EV"), silu_ev(g - 2))
                            lsG.inc(s.dma_start(out=dtb[g % 2][:, :],
                                                 in_=dtf_d.ap()[d8 * 128:(d8 + 1) * 128, c * TSC:(c + 1) * TSC]), g)
                            lsG.inc(s.dma_start(out=ub[g % 2][:, :],
                                                in_=hsf_d.ap()[d8 * 128:(d8 + 1) * 128, c * TSC:(c + 1) * TSC]), g)
                            lsG.inc(s.dma_start(out=gb[g % 2][:, :],
                                                in_=hg_d.ap()[DSL + d8 * 128:DSL + (d8 + 1) * 128, c * TSC:(c + 1) * TSC]), g)
                block.sync(b_sync)

                LDF = p.val("LD")

                def b_scalar(s):
                    s.wait_ge(p.sem("LD"), LDF)
                    p.inc(s.activation(out=a_neg[:, :], in_=a_sb[:, :], func=AF.Exp), "EV", 1)
                    s.wait_ge(p.sem("DV"), V0 + 1)
                    for c in range(NCH):
                        for d8 in range(8):
                            g = c * 8 + d8
                            lsG.wait(s, g)
                            for n in range(NST):
                                i = g * 16 + n
                                if i >= 2:
                                    g2, n2 = (i - 2) // 16, (i - 2) % 16
                                    s.wait_ge(p.sem("DV"), sdv(g2, n2))
                                p.inc(s.activation(out=dA[i % 2][:, :], in_=dtb[g % 2][:, :],
                                                   func=AF.Exp, scale=a_neg[:, d8 * 16 + n:d8 * 16 + n + 1]),
                                      "EV", 1)
                            if g >= 2:
                                s.wait_ge(p.sem("DV"), yfin_dv(g - 2))
                            p.inc(s.activation(out=sg[g % 2][:, :], in_=gb[g % 2][:, :],
                                               func=AF.Silu), "EV", 1)
                            if g >= 1:
                                s.wait_ge(p.sem("DV"), yfin_dv(g - 1))
                                lsY.inc(s.dma_start(
                                    out=yf_d.ap().rearrange("(a q) t -> a q t", q=128)[
                                        (g - 1) % 8, :, ((g - 1) // 8) * TSC:((g - 1) // 8 + 1) * TSC],
                                    in_=yo[(g - 1) % 2][:, :]), g - 1)
                            if g == NCH * 8 - 1:
                                s.wait_ge(p.sem("DV"), yfin_dv(g))
                                lsY.inc(s.dma_start(
                                    out=yf_d.ap().rearrange("(a q) t -> a q t", q=128)[
                                        g % 8, :, (g // 8) * TSC:(g // 8 + 1) * TSC],
                                    in_=yo[g % 2][:, :]), g)
                block.scalar(b_scalar)

                def b_vector(v):
                    v.wait_ge(p.sem("LD"), LDF)
                    v.wait_ge(p.sem("EV"), E0 + 1)
                    p.inc(v.tensor_scalar(out=a_neg[:, :], in0=a_neg[:, :], scalar1=-1.0,
                                          scalar2=None, op0=ALU.mult), "DV", 1)
                    p.inc(v.memset(carry[:, :], 0.0), "DV", 1)
                    for c in range(NCH):
                        for d8 in range(8):
                            g = c * 8 + d8
                            lsG.wait(v, g)
                            p.inc(v.tensor_tensor(out=dtu[:, :], in0=dtb[g % 2][:, :],
                                                  in1=ub[g % 2][:, :], op=ALU.mult), "DV", 1)
                            for n in range(NST):
                                i = g * 16 + n
                                cyc = carry[:, d8 * 16 + n:d8 * 16 + n + 1]
                                v.wait_ge(p.sem("EV"), adv(g, n))
                                v.wait_ge(p.sem("PL"), bdu_pl(g, n))
                                if i >= 2:
                                    g2, n2 = (i - 2) // 16, (i - 2) % 16
                                    v.wait_ge(p.sem("PL"), tmp_pl(g2, n2))
                                init = 0.0 if c == 0 else cyc
                                p.inc(v.tensor_tensor_scan(out=sbf[i % 2][:, :], data0=dA[i % 2][:, :],
                                                           data1=dBu[i % 2][:, :], initial=init,
                                                           op0=ALU.mult, op1=ALU.add), "DV", 1)
                                p.inc(v.tensor_copy(out=cyc, in_=sbf[i % 2][:, TSC - 1:TSC]), "DV", 1)
                                v.wait_ge(p.sem("PL"), tmp_pl(g, n))
                                if n == 0:
                                    p.inc(v.tensor_copy(out=yac[:, :], in_=tmp[i % 2][:, :]), "DV", 1)
                                else:
                                    p.inc(v.tensor_tensor(out=yac[:, :], in0=yac[:, :],
                                                          in1=tmp[i % 2][:, :], op=ALU.add), "DV", 1)
                            p.inc(v.scalar_tensor_tensor(out=yac[:, :], in0=ub[g % 2][:, :],
                                                         scalar=sb_dpar[:, d8:d8 + 1], in1=yac[:, :],
                                                         op0=ALU.mult, op1=ALU.add), "DV", 1)
                            v.wait_ge(p.sem("EV"), silu_ev(g))
                            if g >= 2:
                                lsY.wait(v, g - 2)
                            p.inc(v.tensor_tensor(out=yo[g % 2][:, :], in0=yac[:, :],
                                                  in1=sg[g % 2][:, :], op=ALU.mult), "DV", 1)
                block.vector(b_vector)

                def b_pool(g_):
                    g_.wait_ge(p.sem("LD"), LDF)
                    g_.wait_ge(p.sem("DV"), V0 + 2)
                    for c in range(NCH):
                        lsB.wait(g_, c)
                        for d8 in range(8):
                            g = c * 8 + d8
                            g_.wait_ge(p.sem("DV"), dtu_dv(g))
                            for n in range(NST):
                                i = g * 16 + n
                                if i >= 2:
                                    g2, n2 = (i - 2) // 16, (i - 2) % 16
                                    g_.wait_ge(p.sem("DV"), sdv(g2, n2))
                                p.inc(g_.tensor_tensor(out=dBu[i % 2][:, :], in0=dtu[:, :],
                                                       in1=Bb[:, n * TSC:(n + 1) * TSC], op=ALU.mult),
                                      "PL", 1)
                                g_.wait_ge(p.sem("DV"), sdv(g, n))
                                if i >= 2:
                                    g2, n2 = (i - 2) // 16, (i - 2) % 16
                                    g_.wait_ge(p.sem("DV"), ydv(g2, n2))
                                p.inc(g_.tensor_tensor(out=tmp[i % 2][:, :], in0=sbf[i % 2][:, :],
                                                       in1=Cb[:, n * TSC:(n + 1) * TSC], op=ALU.mult),
                                      "PL", 1)
                block.gpsimd(b_pool)


            barrier()

        def phase_moe(tag, chh_ap, chx_ap, out_ap):
            with contextlib.ExitStack() as eps:
                E = eps.enter_context
                hn_sb = E(nc.sbuf_tensor(f"{tag}_hn", [128, DT * TC], BF16))
                emit_lnt(tag + "l", lambda i: chh_ap[i * 128:(i + 1) * 128, :], 16, "rms_w",
                         ("sbuf", hn_sb, TC), w_bc=sb_ln2[:, :])
                sync_point()
                rt_sb = E(nc.sbuf_tensor(f"{tag}_rt", [128, 16 * 8], BF16))
                prb = E(nc.sbuf_tensor(f"{tag}_pr", [128, 8], F32))
                tcols = E(nc.sbuf_tensor(f"{tag}_tc", [128, 8], F32))
                wpad = E(nc.sbuf_tensor(f"{tag}_wp", [128, 8], BF16))
                wr_sb = E(nc.sbuf_tensor(f"{tag}_wr", [8, TC], F32))
                sh_sb = E(nc.sbuf_tensor(f"{tag}_sh", [1, TC], F32))
                ps_r = E(nc.psum_tensor(f"{tag}_pr8", [128, 8], F32))
                ps_t8 = E(nc.psum_tensor(f"{tag}_pt8", [8, 128], BF16))

                def r_wld(s):
                    s.wait_ge(p.sem("GD"), cast_after["rt"])
                    for k in range(16):
                        p.inc(s.dma_start(out=rt_sb[:, k * 8:k * 8 + 7],
                                          in_=wb["rt"].ap()[k * 128:(k + 1) * 128, :]), "LD", 16)
                block.sync(r_wld)
                LDF = p.val("LD")
                M0, E0, V0 = p.val("MM"), p.val("EV"), p.val("DV")

                def r_vec0(v):
                    p.inc(v.memset(wpad[:, :], 0.0), "DV", 1)
                    p.inc(v.memset(sh_sb[:, :], 0.125), "DV", 1)
                block.vector(r_vec0)
                V0 = p.val("DV")
                rmm = lambda tm: M0 + tm * 2 + 1
                rtp = lambda tm: M0 + tm * 2 + 2
                rev = lambda tm: E0 + tm * 2 + 1
                rev2 = lambda tm: E0 + (tm + 1) * 2
                rdv = lambda tm: V0 + (tm + 1) * 12

                def r_tensor(t):
                    t.wait_ge(p.sem("LD"), LDF)
                    t.wait_ge(p.sem("DV"), V0)
                    for tm in range(16):
                        if tm >= 1:
                            t.wait_ge(p.sem("DV"), rdv(tm - 1))
                        for k in range(16):
                            mm = t.matmul(ps_r[:, 0:7], hn_sb[:, k * TC + tm * 128:k * TC + (tm + 1) * 128],
                                          rt_sb[:, k * 8:k * 8 + 7], start=(k == 0), stop=(k == 15))
                            if k == 15:
                                p.inc(mm, "MM", 1)
                        t.wait_ge(p.sem("DV"), rdv(tm))
                        if tm >= 1:
                            t.wait_ge(p.sem("EV"), rev2(tm - 1))
                        p.inc(t.transpose(ps_t8[:, :], wpad[:, :], sb_idb[:, :]), "MM", 1)
                block.tensor(r_tensor)

                def r_scalar(sc):
                    for tm in range(16):
                        sc.wait_ge(p.sem("DV"), V0 + tm * 12 + 1)
                        p.inc(sc.activation(out=prb[:, 0:7], in_=prb[:, 0:7],
                                            func=AF.Sigmoid), "EV", 1)
                        sc.wait_ge(p.sem("MM"), rtp(tm))
                        p.inc(sc.copy(out=wr_sb[0:8, tm * 128:(tm + 1) * 128],
                                      in_=ps_t8[:, :]), "EV", 1)
                block.scalar(r_scalar)

                mbuf = E(nc.sbuf_tensor(f"{tag}_mk", [128, 8], F32))
                tlw = E(nc.sbuf_tensor(f"{tag}_tl", [128, 8], F32))

                def r_vector(v):
                    v.wait_ge(p.sem("LD"), LDF)
                    for tm in range(16):
                        v.wait_ge(p.sem("MM"), rmm(tm))
                        p.inc(v.tensor_tensor(out=prb[:, 0:7], in0=ps_r[:, 0:7],
                                              in1=sb_rtb[:, 0:7], op=ALU.add), "DV", 1)
                        v.wait_ge(p.sem("EV"), rev(tm))
                        p.inc(v.tensor_reduce(out=tcols[:, 0:1], in_=prb[:, 0:7],
                                              axis=mybir.AxisListType.X, op=ALU.max), "DV", 1)
                        p.inc(v.tensor_scalar(out=mbuf[:, 0:7], in0=prb[:, 0:7],
                                              scalar1=tcols[:, 0:1], scalar2=None,
                                              op0=ALU.is_equal), "DV", 1)
                        p.inc(v.scalar_tensor_tensor(out=tlw[:, 0:7], in0=mbuf[:, 0:7],
                                                     scalar=-1e9, in1=prb[:, 0:7],
                                                     op0=ALU.mult, op1=ALU.add), "DV", 1)
                        p.inc(v.tensor_reduce(out=tcols[:, 1:2], in_=tlw[:, 0:7],
                                              axis=mybir.AxisListType.X, op=ALU.max), "DV", 1)
                        p.inc(v.tensor_scalar(out=tlw[:, 0:7], in0=tlw[:, 0:7],
                                              scalar1=tcols[:, 1:2], scalar2=None,
                                              op0=ALU.is_equal), "DV", 1)
                        p.inc(v.tensor_tensor(out=mbuf[:, 0:7], in0=mbuf[:, 0:7],
                                              in1=tlw[:, 0:7], op=ALU.add), "DV", 1)
                        p.inc(v.tensor_tensor(out=tcols[:, 2:3], in0=tcols[:, 0:1],
                                              in1=tcols[:, 1:2], op=ALU.add), "DV", 1)
                        p.inc(v.reciprocal(out=tcols[:, 3:4], in_=tcols[:, 2:3]), "DV", 1)
                        p.inc(v.tensor_scalar(out=tcols[:, 4:5], in0=tcols[:, 3:4],
                                              scalar1=0.125, scalar2=None, op0=ALU.mult), "DV", 1)
                        if tm >= 1:
                            v.wait_ge(p.sem("MM"), rtp(tm - 1))
                        p.inc(v.scalar_tensor_tensor(out=wpad[:, 0:7], in0=prb[:, 0:7],
                                                     scalar=tcols[:, 4:5], in1=mbuf[:, 0:7],
                                                     op0=ALU.mult, op1=ALU.mult), "DV", 1)
                        p.inc(v.nop(), "DV", 1)
                block.vector(r_vector)

                def r_store(s):
                    s.wait_ge(p.sem("EV"), rev2(15))
                    s.wait_ge(p.sem("DV"), V0)
                    p.inc(s.dma_start(out=wt_d.ap()[1:8, :], in_=wr_sb[0:7, :]), "S2", 16)
                    p.inc(s.dma_start(out=wt_d.ap()[0:1, :], in_=sh_sb[:, :]), "S2", 16)
                block.sync(r_store)
                sync_point()
                S2F = p.val("S2")


                wg_sb = [E(nc.sbuf_tensor(f"{tag}_wg{i}", [128, 16 * 128], BF16)) for i in range(2)]
                wu_sb = [E(nc.sbuf_tensor(f"{tag}_wu{i}", [128, 16 * 128], BF16)) for i in range(2)]
                wbc = E(nc.sbuf_tensor(f"{tag}_wbc", [128, TC], F32))
                sgm = [E(nc.sbuf_tensor(f"{tag}_sg{i}", [128, NT], F32)) for i in range(2)]
                hh = [E(nc.sbuf_tensor(f"{tag}_hh{i}", [128, NT], F32)) for i in range(2)]
                hpo = [E(nc.sbuf_tensor(f"{tag}_hp{i}", [128, NT], BF16)) for i in range(2)]
                ps_g = [E(nc.psum_tensor(f"{tag}_pg{i}", [128, NT], F32)) for i in range(2)]
                ps_u = [E(nc.psum_tensor(f"{tag}_pu{i}", [128, NT], F32)) for i in range(2)]

                lsb_ = p.stream("Mb", 1)
                lsg = p.stream("Mg", 2)
                lsh2 = p.stream("Mh", 1)
                M0, E0, V0, P0 = (p.val(x) for x in ["MM", "EV", "DV", "PL"])
                gidx = lambda e, mt, tc: (e * 8 + mt) * 4 + tc
                g_mm = lambda e, mt, tc, i: M0 + gidx(e, mt, tc) * 2 + i + 1
                g_ev = lambda e, mt, tc: E0 + gidx(e, mt, tc) + 1
                g_dv = lambda e, mt, tc: V0 + gidx(e, mt, tc) + 1
                g_pl = lambda e, mt, tc: P0 + gidx(e, mt, tc) + 1

                def g_sync(s):
                    s.wait_ge(p.sem("GD"), cast_after["moeu"])
                    s.wait_ge(p.sem("S2"), S2F)
                    for e in range(8):
                        if e > 0:
                            s.wait_ge(p.sem("PL"), g_pl(e - 1, 7, 3))
                        lsb_.inc(s.dma_start(out=wbc[:, :],
                                             in_=wt_d.ap()[e:e + 1, :].partition_broadcast(128)), e)
                        for mt in range(8):
                            gi = e * 8 + mt
                            if gi >= 2:
                                e2, mt2 = (gi - 2) // 8, (gi - 2) % 8
                                s.wait_ge(p.sem("MM"), g_mm(e2, mt2, 3, 1))
                            src_g = wb["moeg"].ap()[e].rearrange("(k q) c -> k q c", q=128)[
                                :, :, mt * 128:(mt + 1) * 128].rearrange("k q c -> q k c")
                            src_u = wb["moeu"].ap()[e].rearrange("(k q) c -> k q c", q=128)[
                                :, :, mt * 128:(mt + 1) * 128].rearrange("k q c -> q k c")
                            lsg.inc(s.dma_start(out=wg_sb[gi % 2][:, :], in_=src_g), gi)
                            lsg.inc(s.dma_start(out=wu_sb[gi % 2][:, :], in_=src_u), gi)
                block.sync(g_sync)

                def g_tensor(t):
                    for e in range(8):
                        for mt in range(8):
                            gi = e * 8 + mt
                            lsg.wait(t, gi)
                            for tc in range(4):
                                if gidx(e, mt, tc) >= 2:
                                    i2 = gidx(e, mt, tc) - 2
                                    e2, mt2, tc2 = i2 // 32, (i2 // 4) % 8, i2 % 4
                                    t.wait_ge(p.sem("DV"), g_dv(e2, mt2, tc2))
                                    t.wait_ge(p.sem("EV"), g_ev(e2, mt2, tc2))
                                for k in range(16):
                                    mm = t.matmul(ps_g[tc % 2][:, :], wg_sb[gi % 2][:, k * 128:(k + 1) * 128],
                                                  hn_sb[:, k * TC + tc * NT:k * TC + (tc + 1) * NT],
                                                  start=(k == 0), stop=(k == 15))
                                    if k == 15:
                                        p.inc(mm, "MM", 1)
                                for k in range(16):
                                    mm = t.matmul(ps_u[tc % 2][:, :], wu_sb[gi % 2][:, k * 128:(k + 1) * 128],
                                                  hn_sb[:, k * TC + tc * NT:k * TC + (tc + 1) * NT],
                                                  start=(k == 0), stop=(k == 15))
                                    if k == 15:
                                        p.inc(mm, "MM", 1)
                block.tensor(g_tensor)

                def g_scalar(sc):
                    for e in range(8):
                        for mt in range(8):
                            for tc in range(4):
                                gi2 = gidx(e, mt, tc)
                                sc.wait_ge(p.sem("MM"), g_mm(e, mt, tc, 0))
                                if gi2 >= 2:
                                    i2 = gi2 - 2
                                    sc.wait_ge(p.sem("DV"), V0 + i2 + 1)
                                p.inc(sc.activation(out=sgm[gi2 % 2][:, :], in_=ps_g[tc % 2][:, :],
                                                    func=AF.Silu), "EV", 1)
                                sc.wait_ge(p.sem("PL"), g_pl(e, mt, tc))
                                lsh2.inc(sc.dma_start(out=hp_d.ap()[e, mt * 128:(mt + 1) * 128,
                                                                    tc * NT:(tc + 1) * NT],
                                                      in_=hpo[gi2 % 2][:, :]), gi2)
                block.scalar(g_scalar)

                def g_vector(v):
                    for e in range(8):
                        for mt in range(8):
                            for tc in range(4):
                                gi2 = gidx(e, mt, tc)
                                v.wait_ge(p.sem("EV"), g_ev(e, mt, tc))
                                v.wait_ge(p.sem("MM"), g_mm(e, mt, tc, 1))
                                if gi2 >= 2:
                                    i2 = gi2 - 2
                                    v.wait_ge(p.sem("PL"), P0 + i2 + 1)
                                p.inc(v.tensor_tensor(out=hh[gi2 % 2][:, :], in0=sgm[gi2 % 2][:, :],
                                                      in1=ps_u[tc % 2][:, :], op=ALU.mult), "DV", 1)
                block.vector(g_vector)

                def g_pool(g_):
                    for e in range(8):
                        lsb_.wait(g_, e)
                        for mt in range(8):
                            for tc in range(4):
                                gi2 = gidx(e, mt, tc)
                                g_.wait_ge(p.sem("DV"), g_dv(e, mt, tc))
                                if gi2 >= 2:
                                    lsh2.wait(g_, gi2 - 2)
                                p.inc(g_.tensor_tensor(out=hpo[gi2 % 2][:, :], in0=hh[gi2 % 2][:, :],
                                                       in1=wbc[:, tc * NT:(tc + 1) * NT], op=ALU.mult),
                                      "PL", 1)
                block.gpsimd(g_pool)

                sync_point()

            barrier()
            with contextlib.ExitStack() as eps:
                E = eps.enter_context
                hp_sb = E(nc.sbuf_tensor(f"{tag}_hps", [128, 64 * NT], BF16))
                wd_sb = [E(nc.sbuf_tensor(f"{tag}_wd{i}", [128, 64 * 128], BF16)) for i in range(2)]
                stg2 = [E(nc.sbuf_tensor(f"{tag}_st{i}", [128, NT], F32)) for i in range(2)]
                rx = [E(nc.sbuf_tensor(f"{tag}_rx{i}", [128, NT], F32)) for i in range(2)]
                rh = [E(nc.sbuf_tensor(f"{tag}_rh{i}", [128, NT], F32)) for i in range(2)]
                ob2 = [E(nc.sbuf_tensor(f"{tag}_ob{i}", [128, NT], F32)) for i in range(2)]
                ps_d = [E(nc.psum_tensor(f"{tag}_pd{i}", [128, NT], F32)) for i in range(2)]
                ps_tt = [E(nc.psum_tensor(f"{tag}_pt{i}", [128, NT], F32)) for i in range(2)]

                lsp = p.stream("Mp", 8)
                lsw = p.stream("Mw", 8)
                lse2 = p.stream("Me", 2)
                lso = p.stream("Mo", 1)
                M0, E0, V0 = (p.val(x) for x in ["MM", "EV", "DV"])
                d_mm = lambda tc, m2: M0 + (tc * 16 + m2) * 2 + 1
                d_tp = lambda tc, m2: M0 + (tc * 16 + m2) * 2 + 2
                d_ev = lambda tc, m2: E0 + tc * 16 + m2 + 1
                d_dv = lambda tc, m2: V0 + (tc * 16 + m2 + 1) * 2

                def tdag2(buf, tc, m2):
                    return buf.rearrange("(c j q) d -> c q j d", j=4, q=128)[
                        tc, :, :, m2 * 128:(m2 + 1) * 128].rearrange("q j d -> q j d")

                def d_sync(s):
                    s.wait_ge(p.sem("GD"), cast_after["moed"])
                    for tc in range(4):
                        if tc >= 1:
                            s.wait_ge(p.sem("MM"), d_mm(tc - 1, 15))
                        for e in range(8):
                            src = hp_d.ap()[e].rearrange("(k q) t -> k q t", q=128)[
                                :, :, tc * NT:(tc + 1) * NT].rearrange("k q t -> q k t")
                            lsp.inc(s.dma_start(out=hp_sb[:, e * 8 * NT:(e + 1) * 8 * NT], in_=src),
                                    tc)
                        for m2 in range(16):
                            gi = tc * 16 + m2
                            if gi >= 2:
                                i2 = gi - 2
                                s.wait_ge(p.sem("MM"), d_mm(i2 // 16, i2 % 16))
                            for e in range(8):
                                src = wb["moed"].ap()[e].rearrange("(k q) c -> k q c", q=128)[
                                    :, :, m2 * 128:(m2 + 1) * 128].rearrange("k q c -> q k c")
                                lsw.inc(s.dma_start(out=wd_sb[gi % 2][:, e * 8 * 128:(e + 1) * 8 * 128],
                                                    in_=src), gi)
                            if gi >= 2:
                                i2 = gi - 2
                                s.wait_ge(p.sem("DV"), d_dv(i2 // 16, i2 % 16))
                            lse2.inc(s.dma_start(out=rx[gi % 2][:, :], in_=tdag2(chx_ap, tc, m2)), gi)
                            lse2.inc(s.dma_start(out=rh[gi % 2][:, :], in_=tdag2(chh_ap, tc, m2)), gi)
                block.sync(d_sync)

                def d_tensor(t):
                    for tc in range(4):
                        lsp.wait(t, tc)
                        for m2 in range(16):
                            gi = tc * 16 + m2
                            lsw.wait(t, gi)
                            if gi >= 2:
                                i2 = gi - 2
                                t.wait_ge(p.sem("EV"), d_ev(i2 // 16, i2 % 16))
                                t.wait_ge(p.sem("DV"), d_dv(i2 // 16, i2 % 16))
                            for e in range(8):
                                for hk in range(8):
                                    ki = e * 8 + hk
                                    mm = t.matmul(ps_d[gi % 2][:, :],
                                                  wd_sb[gi % 2][:, ki * 128:(ki + 1) * 128],
                                                  hp_sb[:, e * 8 * NT + hk * NT:e * 8 * NT + (hk + 1) * NT],
                                                  start=(ki == 0), stop=(ki == 63))
                                    if ki == 63:
                                        p.inc(mm, "MM", 1)
                            t.wait_ge(p.sem("EV"), d_ev(tc, m2))
                            for jj in range(4):
                                mm = t.transpose(ps_tt[gi % 2][:, jj * 128:(jj + 1) * 128],
                                                 stg2[gi % 2][:, jj * 128:(jj + 1) * 128], sb_idf[:, :])
                                if jj == 3:
                                    p.inc(mm, "MM", 1)
                block.tensor(d_tensor)

                def d_scalar(sc):
                    for tc in range(4):
                        for m2 in range(16):
                            gi = tc * 16 + m2
                            sc.wait_ge(p.sem("MM"), d_mm(tc, m2))
                            if gi >= 2:
                                i2 = gi - 2
                                sc.wait_ge(p.sem("MM"), d_tp(i2 // 16, i2 % 16))
                            p.inc(sc.copy(out=stg2[gi % 2][:, :], in_=ps_d[gi % 2][:, :]), "EV", 1)
                            sc.wait_ge(p.sem("DV"), d_dv(tc, m2))
                            lso.inc(sc.dma_start(out=tdag2(out_ap, tc, m2), in_=ob2[gi % 2][:, :]),
                                    gi)
                block.scalar(d_scalar)

                def d_vector(v):
                    for tc in range(4):
                        for m2 in range(16):
                            gi = tc * 16 + m2
                            v.wait_ge(p.sem("MM"), d_tp(tc, m2))
                            lse2.wait(v, gi)
                            if gi >= 2:
                                lso.wait(v, gi - 2)
                            p.inc(v.tensor_tensor(out=ob2[gi % 2][:, :], in0=ps_tt[gi % 2][:, :],
                                                  in1=rx[gi % 2][:, :], op=ALU.add), "DV", 1)
                            p.inc(v.tensor_tensor(out=ob2[gi % 2][:, :], in0=ob2[gi % 2][:, :],
                                                  in1=rh[gi % 2][:, :], op=ALU.add), "DV", 1)
                block.vector(d_vector)
            barrier()

        def rs_ag(chunk_buf, full_buf=None):
            sync_point()

            def b(g):
                p.inc(g.collective_compute("ReduceScatter", ALU.add, replica_groups=GROUPS,
                                           ins=[part_d.ap().opt()], outs=[chunk_buf.ap().opt()]),
                      "CC", 1)
                if full_buf is not None:
                    g.wait_ge(p.sem("CC"), p.cnt["CC"])
                    p.inc(g.collective_compute("AllGather", ALU.bypass, replica_groups=GROUPS,
                                               ins=[chunk_buf.ap().opt()], outs=[full_buf.ap().opt()]),
                          "CC", 1)
            block.gpsimd(b)
            barrier()

        def ag_only(chunk_ap, full_buf):
            sync_point()

            def b(g):
                p.inc(g.collective_compute("AllGather", ALU.bypass, replica_groups=GROUPS,
                                           ins=[chunk_ap.opt()], outs=[full_buf.ap().opt()]),
                      "CC", 1)
            block.gpsimd(b)
            barrier()

        ffull = lambda buf: (lambda k, ct: buf.ap()[k * 128:(k + 1) * 128, ct * NT:(ct + 1) * NT])

        emit_lnt("mln", lambda i: xTm[i * 128:(i + 1) * 128, :], 64, "ln",
                 ("dram", xlnf_d.ap()))
        sync_point()
        emit_proj("ipj", wb["inproj"].ap(), 16, 16, ffull(xlnf_d),
                  lambda ct, m: hg_d.ap()[m * 128:(m + 1) * 128, ct * NT:(ct + 1) * NT],
                  cast_key="inproj", evac_bf=False)
        barrier()
        phase_conv()
        barrier()
        phase_ssmp()
        phase_dtbc()
        phase_scan()
        emit_partial("opj", wb["outproj"].ap(), 8, 16, ffull(yf_d), xTm, part_d.ap(),
                     cast_key="outproj")
        rs_ag(ch_x, xf_a)

        def ffn_block(tag, src_full, out_chunk_ap, out_full):
            emit_lnt(tag + "zn", lambda i: src_full.ap()[i * 128:(i + 1) * 128, :], 64,
                     "ln_rms_w", ("dram", znf_d.ap()), z_dst=zf_d.ap(), w_bc=sb_ln1[:, :])
            sync_point()
            emit_proj(tag + "kv", wb["kvd"].ap(), 16, 4, ffull(znf_d),
                      lambda ct, m: kvdf_d.ap()[m * 128:(m + 1) * 128, ct * NT:(ct + 1) * NT],
                      cast_key="kvd")
            sync_point()
            emit_proj(tag + "qd", wb["qd"].ap(), 16, 4, ffull(znf_d),
                      lambda ct, m: qdf_d.ap()[m * 128:(m + 1) * 128, ct * NT:(ct + 1) * NT],
                      cast_key="qd")
            sync_point()
            attention(tag + "at", "mla")
            emit_partial(tag + "ow", wb["ow"].ap(), 4, 16, ffull(oh_d), zf_d.ap(), part_d.ap(),
                         cast_key="ow")
            rs_ag(ch_h)
            phase_moe(tag + "mo", ch_h.ap(), ch_x.ap(), out_chunk_ap)
            if out_full is not None:
                ag_only(out_chunk_ap, out_full)

        if N_SUB >= 2:
            x2c = dscr("x2c", [TC, D])
            ffn_block("f1", xf_a, x2c.ap(), xf_b)

        if N_SUB >= 3:
            emit_lnt("3zn", lambda i: xf_b.ap()[i * 128:(i + 1) * 128, :], 64, "ln",
                     ("dram", znf_d.ap()))
            sync_point()
            attention("3at", "mha")
            emit_partial("3wo", wb["wout"].ap(), 4, 16, ffull(oh_d), xf_b.ap(), part_d.ap(),
                         cast_key="wout")
            rs_ag(ch_x, xf_a)

        if N_SUB >= 4:
            ffn_block("f2", xf_a, y_out, None)

        if N_SUB < 4:
            dbg_src = {1: xf_a, 2: xf_b, 3: xf_a}[N_SUB]

            def dbg(s):
                p.inc(s.dma_start(out=y_dbg, in_=dbg_src.ap()), "ST", 16)
            block.sync(dbg)

        sync_point()
    return nc


def kernel(**inputs):
    in_maps = make_inmaps(inputs)
    nc = build_nc()
    res = run_bass_kernel_spmd(nc, in_maps, core_ids=list(range(8)))
    out = np.zeros((B, S, D), np.float32)
    if N_SUB >= 4:
        for c in range(8):
            b, r = c // 4, c % 4
            out[b, r * TC:(r + 1) * TC, :] = res.results[c]["y"]
    else:
        for c in range(0, 8, 4):
            out[c // 4] = res.results[c]["y_dbg"]
    return out
